# version 11
# speedup vs baseline: 1.6769x; 1.0242x over previous
import numpy as np
import ml_dtypes
from contextlib import ExitStack

import concourse.bass as bass
import concourse.mybir as mybir
from concourse.bass_utils import run_bass_kernel_spmd

F32 = mybir.dt.float32
BF16 = mybir.dt.bfloat16
AF = mybir.ActivationFunctionType
ALU = mybir.AluOpType
AX = mybir.AxisListType

S, D, H, DH = 2048, 1024, 8, 128
C = 128
GRP = 256
NG = S // GRP
DFF = 4096
NTT = S // 256
PI = float(np.pi)

ENGINES = ("sync", "tensor", "scalar", "vector", "gpsimd")


class Tok:
    __slots__ = ("sem_name", "order", "used", "val")

    def __init__(self, sem_name, order):
        self.sem_name = sem_name
        self.order = order
        self.used = False
        self.val = None


class Prog:

    def __init__(self, nc, ctx, usage=None):
        self.nc = nc
        self.phase2 = usage is not None
        self.usage = usage if usage is not None else set()
        if self.phase2:
            self.sems = {e: ctx.enter_context(nc.semaphore(f"sm_{e}")) for e in ENGINES}
            for j in range(4):
                self.sems[f"dma{j}"] = ctx.enter_context(nc.semaphore(f"sm_dma{j}"))
            for j in range(2):
                self.sems[f"odma{j}"] = ctx.enter_context(nc.semaphore(f"sm_odma{j}"))
            for j in range(2):
                self.sems[f"gdma{j}"] = ctx.enter_context(nc.semaphore(f"sm_gdma{j}"))
        else:
            self.sems = {e: None for e in ENGINES}
            for j in range(4):
                self.sems[f"dma{j}"] = None
            for j in range(2):
                self.sems[f"odma{j}"] = None
            for j in range(2):
                self.sems[f"gdma{j}"] = None
        self.cnt = {k: 0 for k in self.sems}
        self.order_ctr = {k: 0 for k in self.sems}
        self.last_tok = {k: None for k in self.sems}
        self.W = {}
        self.R = {}
        self.lists = {e: [] for e in ENGINES}
        self.mark = {e: {k: 0 for k in self.sems} for e in ENGINES}
        self.seen = set()
        self.fence_vals = None
        self.n_incs = 0

    def _wait(self, eng, tok):
        if isinstance(tok, Tok):
            if eng == "tensor" and tok.sem_name == "tensor":
                return
            if self.mark[eng][tok.sem_name] >= tok.order:
                return
            self.mark[eng][tok.sem_name] = tok.order
            if not self.phase2:
                tok.used = True
                self.usage.add((tok.sem_name, tok.order))
                return
            assert tok.used and tok.val is not None, (eng, tok.sem_name, tok.order)
            sem = self.sems[tok.sem_name]
            val = tok.val
            self.lists[eng].append(lambda e, sem=sem, val=val: e.wait_ge(sem, val))
        else:
            sem_name, val = tok
            if not self.phase2 or val <= 0:
                return
            key = ("val", sem_name)
            cur = self.mark[eng].get(key, 0)
            if cur >= val:
                return
            self.mark[eng][key] = val
            sem = self.sems[sem_name]
            self.lists[eng].append(lambda e, sem=sem, val=val: e.wait_ge(sem, val))

    def fence(self):
        for k, t in self.last_tok.items():
            if t is not None and not self.phase2:
                t.used = True
                self.usage.add((t.sem_name, t.order))
        self.fence_vals = {k: self.cnt[k] for k in self.cnt}

    def op(self, eng, fn, reads=(), writes=(), sem=None, inc=1, extra_waits=(),
           force_inc=False):
        reads = list(reads)
        writes = list(writes)
        for t in extra_waits:
            self._wait(eng, t)
        for k in reads + writes:
            if k not in self.seen:
                self.seen.add(k)
                if self.fence_vals is not None:
                    for sname, val in self.fence_vals.items():
                        self._wait(eng, (sname, val))
        for k in reads:
            t = self.W.get(k)
            if t is not None:
                self._wait(eng, t)
            if isinstance(k, tuple) and len(k) == 2 and k[0] == "pb":
                sem_name = sem or eng
                for t in self.R.get(k, ()):
                    if t.sem_name != sem_name:
                        self._wait(eng, t)
        for k in writes:
            t = self.W.get(k)
            if t is not None:
                self._wait(eng, t)
            for t in self.R.get(k, ()):
                self._wait(eng, t)
        sem_name = sem or eng
        self.order_ctr[sem_name] += 1
        tok = Tok(sem_name, self.order_ctr[sem_name])
        self.last_tok[sem_name] = tok
        if force_inc and not self.phase2:
            tok.used = True
            self.usage.add((sem_name, tok.order))
        if self.phase2:
            if (sem_name, tok.order) in self.usage:
                tok.used = True
                self.cnt[sem_name] += inc
                tok.val = self.cnt[sem_name]
                self.n_incs += 1
            semobj = self.sems[sem_name]

            def closure(e, fn=fn, semobj=semobj, inc=inc, tok=tok):
                inst = fn(e)
                if tok.used:
                    inst.then_inc(semobj, inc)

            self.lists[eng].append(closure)
        for k in writes:
            self.W[k] = tok
            self.R[k] = []
        for k in reads:
            self.R.setdefault(k, []).append(tok)
        return tok

    def dma_in(self, dst, src, reads=(), writes=()):
        j = self.order_ctr.get("_dma_rr", 0)
        self.order_ctr["_dma_rr"] = j + 1
        sem_name = f"dma{j % 4}"
        prev = self.last_tok[sem_name]
        return self.op("sync", lambda e: e.dma_start(dst, src),
                       reads=reads, writes=writes, sem=sem_name, inc=16,
                       extra_waits=[prev] if prev is not None else [])

    def dma_cast(self, dst, src, reads=(), writes=(), chain=True, sem_idx=None,
                 force_inc=False):
        if sem_idx is None:
            j = self.order_ctr.get("_gdma_rr", 0)
            self.order_ctr["_gdma_rr"] = j + 1
            sem_name = f"gdma{j % 2}"
        else:
            sem_name = f"gdma{sem_idx}"
        prev = self.last_tok[sem_name]
        return self.op("gpsimd", lambda e: e.dma_start(dst, src),
                       reads=reads, writes=writes, sem=sem_name, inc=16,
                       extra_waits=[prev] if (chain and prev is not None) else [],
                       force_inc=force_inc)

    def dma_out(self, dst, src, reads=(), writes=()):
        j = self.order_ctr.get("_odma_rr", 0)
        self.order_ctr["_odma_rr"] = j + 1
        sem_name = f"odma{j % 2}"
        prev = self.last_tok[sem_name]
        return self.op("scalar", lambda e: e.dma_start(dst, src),
                       reads=reads, writes=writes, sem=sem_name, inc=16,
                       extra_waits=[prev] if prev is not None else [])

    def final_join(self):
        for k, t in self.last_tok.items():
            if t is not None and not self.phase2:
                t.used = True
                self.usage.add((t.sem_name, t.order))
        tok = {k: self.cnt[k] for k in self.cnt if not k.startswith("_")}
        for e in ENGINES:
            for sname, val in tok.items():
                self._wait(e, (sname, val))

    def collect_usage(self):
        return self.usage

    def flush(self, blk):
        if not self.phase2:
            for e in ENGINES:
                self.lists[e] = []
            return
        for e in ENGINES:
            lst = self.lists[e]
            if not lst:
                continue

            def body(engine, lst=lst):
                for fn in lst:
                    fn(engine)

            getattr(blk, e)(body)
            self.lists[e] = []


GAMMAS = 1.0 - np.exp2(-5.0 - np.arange(8, dtype=np.float64))
GAMMA_C = [float(g ** 128) for g in GAMMAS]


def host_constants():
    cst = {}
    cst["ones_row"] = np.ones((1, 128), np.float32)
    m = (np.arange(128)[None, :] >= np.arange(128)[:, None]).astype(np.float32)
    cst["mask4_t"] = np.ascontiguousarray(np.tile(m, (1, 4)))
    cst["id_bf"] = np.eye(128, dtype=np.float32).astype(ml_dtypes.bfloat16)
    cst["id_f32"] = np.eye(128, dtype=np.float32)
    cst["halfpi_col"] = np.full((128, 1), np.pi / 2.0, np.float32)
    cst["tenth_col"] = np.full((128, 1), 0.1, np.float32)
    n = np.arange(128, dtype=np.float64)
    qs = np.empty((1, 2048), np.float64)
    ks = np.empty((1, 2048), np.float64)
    for h in range(8):
        qs[0, h * 256:h * 256 + 128] = GAMMAS[h] ** n / np.sqrt(128.0)
        qs[0, h * 256 + 128:h * 256 + 256] = qs[0, h * 256:h * 256 + 128]
        ks[0, h * 256:h * 256 + 128] = GAMMAS[h] ** (-n)
        ks[0, h * 256 + 128:h * 256 + 256] = ks[0, h * 256:h * 256 + 128]
    cst["qscale_t"] = np.ascontiguousarray(
        np.broadcast_to(qs, (128, 2048))).astype(ml_dtypes.bfloat16)
    cst["kscale_t"] = np.ascontiguousarray(
        np.broadcast_to(ks, (128, 2048))).astype(ml_dtypes.bfloat16)
    return cst


def build_program(debug=False, upto="B"):
    usage = _emit(bass.Bass("TRN2", target_bir_lowering=False), None, debug, upto)
    nc = bass.Bass("TRN2", target_bir_lowering=False)
    _emit(nc, usage, debug, upto)
    return nc


def _emit(nc, usage, debug, upto="B"):

    X_h = nc.dram_tensor("L0_attn_X", [S, D], F32, kind="ExternalInput").ap()
    Wq_h = nc.dram_tensor("L0_attn_Wq", [D, D], F32, kind="ExternalInput").ap()
    Wk_h = nc.dram_tensor("L0_attn_Wk", [D, D], F32, kind="ExternalInput").ap()
    Wv_h = nc.dram_tensor("L0_attn_Wv", [D, D], F32, kind="ExternalInput").ap()
    Wg_h = nc.dram_tensor("L0_attn_W_G", [D, D], F32, kind="ExternalInput").ap()
    Wo_h = nc.dram_tensor("L0_attn_W_O", [D, D], F32, kind="ExternalInput").ap()
    W1_h = nc.dram_tensor("L0_ffn_W1", [D, DFF], F32, kind="ExternalInput").ap()
    W2_h = nc.dram_tensor("L0_ffn_W2", [DFF, D], F32, kind="ExternalInput").ap()
    b1_h = nc.dram_tensor("L0_ffn_b1", [32, 128], F32, kind="ExternalInput").ap()
    th_h = nc.dram_tensor("L0_ffn_quantum_phases", [32, 128], F32, kind="ExternalInput").ap()
    wr_h = nc.dram_tensor("L0_ffn_quantum_weights_real", [32, 128], F32, kind="ExternalInput").ap()
    wi_h = nc.dram_tensor("L0_ffn_quantum_weights_imag", [32, 128], F32, kind="ExternalInput").ap()
    b2_h = nc.dram_tensor("L0_ffn_b2", [8, 128], F32, kind="ExternalInput").ap()
    c_h = nc.dram_tensor("L0_c", [8, 128], F32, kind="ExternalInput").ap()
    wm_h = nc.dram_tensor("L0_wmod", [D, 6 * D], F32, kind="ExternalInput").ap()
    cst_h = {}
    for name, arr in host_constants().items():
        dt = BF16 if arr.dtype == ml_dtypes.bfloat16 else F32
        cst_h[name] = nc.dram_tensor(name, list(arr.shape), dt, kind="ExternalInput").ap()
    out_h = nc.dram_tensor("OUT", [S, D], F32, kind="ExternalOutput").ap()
    dbg_h = {}
    if debug:
        for nm, shp, dt_ in [("d_silu", [128, 8], F32), ("d_modc", [128, 48], F32),
                             ("d_ga", [128, D], F32), ("d_ret", [S, D], BF16),
                             ("d_mbf", [S, D], BF16), ("d_gated", [S, D], BF16),
                             ("d_h1", [D, S], BF16),
                             ("d_qt", [D, S], BF16), ("d_kt", [D, S], BF16),
                             ("d_v", [S, D], BF16), ("d_g", [S, D], BF16),
                             ("d_x2", [S, D], BF16), ("d_h2", [D, S], BF16),
                             ("d_pcols", [128, 32], F32), ("d_rcols", [128, 32], F32)]:
            dbg_h[nm] = nc.dram_tensor(nm, shp, dt_, kind="ExternalOutput").ap()

    with ExitStack() as ctx:
        P = Prog(nc, ctx, usage)

        def alloc(c_, name, pdim, fdim, dtype=F32):
            return c_.enter_context(nc.sbuf_tensor(name, [pdim, fdim], dtype))

        xbf = alloc(ctx, "xbf", 128, 16 * 1024, BF16)
        ones_row = alloc(ctx, "sb_ones_row", 1, 128)
        id_bf = alloc(ctx, "sb_id_bf", 128, 128, BF16)
        halfpi_col = alloc(ctx, "sb_halfpi_col", 128, 1)
        tenth_col = alloc(ctx, "sb_tenth_col", 128, 1)
        ca1_cols = alloc(ctx, "ca1_cols", 128, 8)
        sa_cols = alloc(ctx, "sa_cols", 128, 8)
        cb1_cols = alloc(ctx, "cb1_cols", 128, 8)
        sb_cols = alloc(ctx, "sb_cols", 128, 8)
        gb_tile = alloc(ctx, "gb_tile", 128, 1024)
        gbb2_tile = alloc(ctx, "gbb2_tile", 128, 1024)
        p_cols = alloc(ctx, "p_cols", 128, 32)
        r_cols = alloc(ctx, "r_cols", 128, 32)
        b1_cols = alloc(ctx, "b1_cols", 128, 32)
        wstage = alloc(ctx, "wstage", 128, 1024)
        stats = alloc(ctx, "stats", 128, 48)
        sqscr = alloc(ctx, "sqscr", 128, 1024)

        psum = [ctx.enter_context(nc.psum_tensor(f"pb{i}", [128, 512], F32))
                for i in range(8)]

        ab_ctx = ctx.enter_context(ExitStack())
        mask4_t = alloc(ab_ctx, "sb_mask4_t", 128, 512)
        id_f32 = alloc(ab_ctx, "sb_id_f32", 128, 128)
        ga_tile = alloc(ab_ctx, "ga_tile", 128, 1024)
        qscale_t = alloc(ab_ctx, "sb_qscale_t", 128, 2048, BF16)
        kscale_t = alloc(ab_ctx, "sb_kscale_t", 128, 2048, BF16)
        wbf = {}
        for nm in ["wq", "wk", "wv", "wg", "wo"]:
            wbf[nm] = alloc(ab_ctx, nm + "_bf", 128, 8 * 1024, BF16)
        h1r = alloc(ab_ctx, "h1r", 128, 2048, BF16)
        qtr = alloc(ab_ctx, "qtr", 128, 2 * 2048, BF16)
        ktr = alloc(ab_ctx, "ktr", 128, 2 * 2048, BF16)
        vtr = alloc(ab_ctx, "vtr", 128, 2048, BF16)
        gtr = alloc(ab_ctx, "gtr", 128, 2048, BF16)
        gated = alloc(ab_ctx, "gated", 128, 2048, BF16)
        gfm = alloc(ab_ctx, "gfm", 128, 1024, BF16)
        retn = alloc(ab_ctx, "retn", 128, 1024, BF16)
        ktm = alloc(ab_ctx, "ktm", 128, 1024, BF16)
        abuf = alloc(ab_ctx, "abuf", 128, 1024, BF16)
        Mst = alloc(ab_ctx, "Mst", 128, 1024)
        Mbf = alloc(ab_ctx, "Mbf", 128, 1024, BF16)
        hpre = alloc(ab_ctx, "hpre", 128, 2048, BF16)

        def t_copy(eng, dst, src, reads, writes):
            P.op(eng, lambda e: e.tensor_copy(dst, src), reads=reads, writes=writes)

        def rsqrt_cols(eng_key, region, veps_sl, y_sl, u_sl):
            P.op("scalar", lambda e: e.activation(
                    stats[:, u_sl[0]:u_sl[1]], stats[:, veps_sl[0]:veps_sl[1]],
                    AF.Sqrt),
                 reads=[(eng_key, "veps"), region], writes=[(eng_key, "u"), region])
            P.op("vector", lambda e: e.reciprocal(
                    stats[:, y_sl[0]:y_sl[1]], stats[:, u_sl[0]:u_sl[1]]),
                 reads=[(eng_key, "u"), region], writes=[(eng_key, "y"), region])

        def ln_chain(sk, eps):
            P.op("vector", lambda e: e.tensor_scalar(
                    stats[:, 4:6], stats[:, 0:2], 1.0 / D, None, ALU.mult),
                 reads=[(sk, "s0"), (sk, "s1"), "stats_a"],
                 writes=[(sk, "mu"), "stats_a"])
            P.op("vector", lambda e: e.tensor_scalar(
                    stats[:, 2:4], stats[:, 2:4], 1.0 / D, None, ALU.mult),
                 reads=[(sk, "q0"), (sk, "q1"), "stats_a"],
                 writes=[(sk, "msq"), "stats_a"])
            P.op("vector", lambda e: e.scalar_tensor_tensor(
                    stats[:, 6:8], stats[:, 4:6], -1.0, stats[:, 4:6],
                    ALU.mult, ALU.mult),
                 reads=[(sk, "mu"), "stats_a"], writes=[(sk, "m2"), "stats_a"])
            P.op("vector", lambda e: e.tensor_add(
                    stats[:, 2:4], stats[:, 6:8], stats[:, 2:4]),
                 reads=[(sk, "m2"), (sk, "msq"), "stats_a"],
                 writes=[(sk, "var"), "stats_a"])
            P.op("vector", lambda e: e.tensor_scalar(
                    stats[:, 2:4], stats[:, 2:4], eps, None, ALU.add),
                 reads=[(sk, "var"), "stats_a"], writes=[(sk, "veps"), "stats_a"])
            rsqrt_cols(sk, "stats_a", (2, 4), (6, 8), (0, 2))

        silu_cols = alloc(ab_ctx, "silu_cols", 128, 8)
        wmp = alloc(ab_ctx, "wmp", 128, 2 * 768)
        macc = alloc(ab_ctx, "macc", 128, 48)
        ga_cols = alloc(ab_ctx, "ga_cols", 128, 8)
        gb_cols = alloc(ab_ctx, "gb_cols", 128, 8)
        b2_cols = alloc(ab_ctx, "b2_cols", 128, 8)
        ctmp = alloc(ab_ctx, "ctmp", 128, 8)
        ctmp2 = alloc(ab_ctx, "ctmp2", 128, 8)

        def clamp_cols(dst, dst_key, c0):
            mk = ("macc", c0 // 24)
            P.op("vector", lambda e: e.tensor_scalar(
                    ctmp[:, :], macc[:, c0:c0 + 8], 0.0, None, ALU.is_ge),
                 reads=[mk], writes=["ctmp"])
            P.op("vector", lambda e: e.tensor_scalar(
                    ctmp2[:, :], macc[:, c0:c0 + 8], 0.25, None, ALU.max),
                 reads=[mk], writes=["ctmp2"])
            P.op("vector", lambda e: e.tensor_scalar(
                    dst[:, :], macc[:, c0:c0 + 8], -0.25, None, ALU.min),
                 reads=[mk], writes=[dst_key])
            P.op("vector", lambda e: e.tensor_sub(ctmp2[:, :], ctmp2[:, :], dst[:, :]),
                 reads=["ctmp2", dst_key], writes=["ctmp2"])
            P.op("vector", lambda e: e.tensor_mul(ctmp[:, :], ctmp[:, :], ctmp2[:, :]),
                 reads=["ctmp", "ctmp2"], writes=["ctmp"])
            P.op("vector", lambda e: e.tensor_add(dst[:, :], dst[:, :], ctmp[:, :]),
                 reads=["ctmp", dst_key], writes=[dst_key])

        def cols_to_tile(src, src_key, tile, tile_key):
            for rnd in range(2):
                for j in range(4):
                    jj = rnd * 4 + j
                    P.op("tensor", lambda e, jj=jj, j=j: e.matmul(
                            psum[5][0:1, j * 128:(j + 1) * 128],
                            src[:, jj:jj + 1], id_f32[:, :], start=True, stop=True),
                         reads=[src_key, "id_f32"], writes=[("pb", 5)])
                t_copy("vector", tile[0:1, rnd * 512:(rnd + 1) * 512],
                       psum[5][0:1, 0:512], [("pb", 5)], [tile_key])
            for half in (1, 0):
                P.op("tensor", lambda e, half=half: e.matmul(
                        psum[5][:, 0:512], ones_row[:, :],
                        tile[0:1, half * 512:(half + 1) * 512], start=True, stop=True),
                     reads=[tile_key, "ones_row"], writes=[("pb", 5)])
                t_copy("vector", tile[:, half * 512:(half + 1) * 512],
                       psum[5][:, 0:512], [("pb", 5)], [tile_key])

        def emit_matvec_half(half, dks=range(8)):
            for dk in dks:
                for part in range(4):
                    s = (dk * 4 + part) % 2
                    c0h = half * 3072 + part * 768
                    P.dma_in(wmp[:, s * 768:(s + 1) * 768],
                             wm_h[dk * 128:(dk + 1) * 128, c0h:c0h + 768],
                             writes=[("wmp", s)])
                    for c in range(6):
                        ci = part * 6 + c
                        P.op("tensor", lambda e, s=s, c=c, ci=ci, dk=dk: e.matmul(
                                psum[6][:, ci:ci + 1],
                                wmp[:, s * 768 + c * 128:s * 768 + (c + 1) * 128],
                                silu_cols[:, dk:dk + 1],
                                start=True, stop=True),
                             reads=[("wmp", s), "silu_cols"], writes=[("pb", 6)])
                mk = ("macc", half)
                if dk == 0:
                    t_copy("vector", macc[:, half * 24:half * 24 + 24],
                           psum[6][:, 0:24], [("pb", 6)], [mk])
                else:
                    P.op("vector", lambda e, half=half: e.tensor_add(
                            macc[:, half * 24:half * 24 + 24],
                            macc[:, half * 24:half * 24 + 24], psum[6][:, 0:24]),
                         reads=[("pb", 6), mk], writes=[mk])

        def emit_mod_right_final():
            t_copy("vector", sb_cols[:, :], macc[:, 24:32], [("macc", 1)], ["sb_cols"])
            P.op("vector", lambda e: e.tensor_scalar(
                    cb1_cols[:, :], macc[:, 32:40], 1.0, None, ALU.add),
                 reads=[("macc", 1)], writes=["cb1_cols"])
            clamp_cols(gb_cols, "gb_cols", 40)
            cols_to_tile(gb_cols, "gb_cols", gb_tile, "gb_tile")
            P.op("vector", lambda e: e.tensor_mul(ctmp2[:, :], gb_cols[:, :], b2_cols[:, :]),
                 reads=["gb_cols", "b2_cols"], writes=["ctmp2"])
            cols_to_tile(ctmp2, "ctmp2", gbb2_tile, "gbb2_tile")

        with ExitStack() as pctx:
            cstage = alloc(pctx, "cstage", 8, 128)
            b2stage = alloc(pctx, "b2stage", 8, 128)
            vstage = alloc(pctx, "vstage", 32, 4 * 128)
            th_cols = alloc(pctx, "th_cols", 128, 32)
            wr_cols = alloc(pctx, "wr_cols", 128, 32)
            wi_cols = alloc(pctx, "wi_cols", 128, 32)
            beta_cols = alloc(pctx, "beta_cols", 128, 32)
            tmpc = alloc(pctx, "tmpc", 128, 32)
            tmpc2 = alloc(pctx, "tmpc2", 128, 32)
            sth_cols = alloc(pctx, "sth_cols", 128, 32)
            cth_cols = alloc(pctx, "cth_cols", 128, 32)
            a_cols = alloc(pctx, "a_cols", 128, 32)

            for name, dst in [("ones_row", ones_row), ("mask4_t", mask4_t),
                              ("id_bf", id_bf), ("id_f32", id_f32),
                              ("halfpi_col", halfpi_col), ("tenth_col", tenth_col),
                              ("qscale_t", qscale_t), ("kscale_t", kscale_t)]:
                P.dma_in(dst[:, :], cst_h[name], writes=[name])
            P.dma_in(cstage[:, :], c_h, writes=["cstage"])
            P.dma_in(b2stage[:, :], b2_h, writes=["b2stage"])
            P.dma_in(vstage[:, 0:128], b1_h, writes=["vs_b1"])
            P.dma_in(vstage[:, 128:256], th_h, writes=["vs_th"])
            P.dma_in(vstage[:, 256:384], wr_h, writes=["vs_wr"])
            P.dma_in(vstage[:, 384:512], wi_h, writes=["vs_wi"])

            P.op("tensor", lambda e: e.matmul(psum[7][:, 0:8], cstage[:, :],
                                              id_f32[0:8, 0:8], start=True, stop=True),
                 reads=["cstage", "id_f32"], writes=[("pb", 7)])
            P.op("scalar", lambda e: e.activation(silu_cols[:, :], psum[7][:, 0:8], AF.Silu),
                 reads=[("pb", 7)], writes=["silu_cols"])
            P.op("tensor", lambda e: e.matmul(psum[7][:, 8:16], b2stage[:, :],
                                              id_f32[0:8, 0:8], start=True, stop=True),
                 reads=["b2stage", "id_f32"], writes=[("pb", 7)])
            t_copy("vector", b2_cols[:, :], psum[7][:, 8:16], [("pb", 7)], ["b2_cols"])
            for j, key in enumerate(["vs_b1", "vs_th", "vs_wr", "vs_wi"]):
                P.op("tensor", lambda e, j=j: e.matmul(
                        psum[7][:, 32 + j * 32:32 + (j + 1) * 32],
                        vstage[:, j * 128:(j + 1) * 128], id_f32[0:32, 0:32],
                        start=True, stop=True),
                     reads=[key, "id_f32"], writes=[("pb", 7)])
            t_copy("vector", b1_cols[:, :], psum[7][:, 32:64], [("pb", 7)], ["b1_cols"])
            t_copy("vector", th_cols[:, :], psum[7][:, 64:96], [("pb", 7)], ["th_cols"])
            t_copy("vector", wr_cols[:, :], psum[7][:, 96:128], [("pb", 7)], ["wr_cols"])
            t_copy("vector", wi_cols[:, :], psum[7][:, 128:160], [("pb", 7)], ["wi_cols"])

            emit_matvec_half(0)
            for i in range(2):
                P.dma_cast(xbf[:, i * 1024:(i + 1) * 1024],
                           X_h[i * 128:(i + 1) * 128, :], writes=[("xbf", i)])
            wsrc = {"wq": Wq_h, "wk": Wk_h, "wv": Wv_h, "wg": Wg_h}
            for nm in ["wq", "wk", "wv", "wg"]:
                for dk in range(8):
                    P.dma_cast(wbf[nm][:, dk * 1024:(dk + 1) * 1024],
                               wsrc[nm][dk * 128:(dk + 1) * 128, :],
                               writes=[(nm, dk)])
            for i in range(2, 16):
                P.dma_cast(xbf[:, i * 1024:(i + 1) * 1024],
                           X_h[i * 128:(i + 1) * 128, :], writes=[("xbf", i)])

            if debug:
                P.dma_out(dbg_h["d_silu"], silu_cols[:, :], reads=["silu_cols"])
                P.dma_out(dbg_h["d_modc"][:, 0:24], macc[:, 0:24], reads=[("macc", 0)])
            t_copy("vector", sa_cols[:, :], macc[:, 0:8], [("macc", 0)], ["sa_cols"])
            P.op("vector", lambda e: e.tensor_scalar(
                    ca1_cols[:, :], macc[:, 8:16], 1.0, None, ALU.add),
                 reads=[("macc", 0)], writes=["ca1_cols"])
            clamp_cols(ga_cols, "ga_cols", 16)
            cols_to_tile(ga_cols, "ga_cols", ga_tile, "ga_tile")

            for dk in range(8):
                for hf in range(2):
                    stg = (dk * 2 + hf) % 2
                    P.dma_in(wstage[:, stg * 512:(stg + 1) * 512],
                             Wo_h[dk * 128:(dk + 1) * 128, hf * 512:(hf + 1) * 512],
                             writes=[("wstg", stg)])
                    P.op("vector", lambda e, dk=dk, stg=stg, hf=hf: e.tensor_mul(
                            wbf["wo"][:, dk * 1024 + hf * 512:dk * 1024 + (hf + 1) * 512],
                            wstage[:, stg * 512:(stg + 1) * 512],
                            ga_tile[:, hf * 512:(hf + 1) * 512]),
                         reads=[("wstg", stg), "ga_tile"], writes=[("wo", dk, hf)])

            def reduce_range(key, buf):
                for _ in range(8):
                    P.op("vector", lambda e: e.tensor_scalar(
                            tmpc[:, :], buf[:, :], PI, None, ALU.is_gt),
                         reads=[key], writes=["tmpc"])
                    P.op("vector", lambda e: e.scalar_tensor_tensor(
                            buf[:, :], tmpc[:, :], -2.0 * PI, buf[:, :], ALU.mult, ALU.add),
                         reads=["tmpc", key], writes=[key])
                    P.op("vector", lambda e: e.tensor_scalar(
                            tmpc[:, :], buf[:, :], -PI, None, ALU.is_lt),
                         reads=[key], writes=["tmpc"])
                    P.op("vector", lambda e: e.scalar_tensor_tensor(
                            buf[:, :], tmpc[:, :], 2.0 * PI, buf[:, :], ALU.mult, ALU.add),
                         reads=["tmpc", key], writes=[key])

            def cos_of(key, buf, outbuf, outkey):
                P.op("vector", lambda e: e.tensor_scalar(
                        tmpc2[:, :], buf[:, :], PI / 2.0, None, ALU.add),
                     reads=[key], writes=["tmpc2"])
                P.op("vector", lambda e: e.tensor_scalar(
                        tmpc[:, :], tmpc2[:, :], PI, None, ALU.is_gt),
                     reads=["tmpc2"], writes=["tmpc"])
                P.op("vector", lambda e: e.scalar_tensor_tensor(
                        tmpc2[:, :], tmpc[:, :], -2.0 * PI, tmpc2[:, :], ALU.mult, ALU.add),
                     reads=["tmpc", "tmpc2"], writes=["tmpc2"])
                P.op("scalar", lambda e: e.activation(outbuf[:, :], tmpc2[:, :], AF.Sin),
                     reads=["tmpc2"], writes=[outkey])

            P.op("vector", lambda e: e.scalar_tensor_tensor(
                    beta_cols[:, :], b1_cols[:, :], 0.1, th_cols[:, :], ALU.mult, ALU.add),
                 reads=["b1_cols", "th_cols"], writes=["beta_cols"])
            reduce_range("beta_cols", beta_cols)
            reduce_range("th_cols", th_cols)
            P.op("scalar", lambda e: e.activation(sth_cols[:, :], th_cols[:, :], AF.Sin),
                 reads=["th_cols"], writes=["sth_cols"])
            cos_of("th_cols", th_cols, cth_cols, "cth_cols")
            P.op("vector", lambda e: e.tensor_mul(cth_cols[:, :], cth_cols[:, :], wr_cols[:, :]),
                 reads=["cth_cols", "wr_cols"], writes=["cth_cols"])
            P.op("vector", lambda e: e.tensor_mul(sth_cols[:, :], sth_cols[:, :], wi_cols[:, :]),
                 reads=["sth_cols", "wi_cols"], writes=["sth_cols"])
            P.op("vector", lambda e: e.tensor_add(a_cols[:, :], cth_cols[:, :], sth_cols[:, :]),
                 reads=["cth_cols", "sth_cols"], writes=["a_cols"])
            P.op("vector", lambda e: e.scalar_tensor_tensor(
                    a_cols[:, :], a_cols[:, :], 0.1, wi_cols[:, :], ALU.mult, ALU.mult),
                 reads=["a_cols", "wi_cols"], writes=["a_cols"])
            P.op("scalar", lambda e: e.activation(sth_cols[:, :], beta_cols[:, :], AF.Sin),
                 reads=["beta_cols"], writes=["sth_cols"])
            cos_of("beta_cols", beta_cols, cth_cols, "cth_cols")
            P.op("vector", lambda e: e.tensor_mul(p_cols[:, :], a_cols[:, :], sth_cols[:, :]),
                 reads=["a_cols", "sth_cols"], writes=["p_cols"])
            P.op("vector", lambda e: e.tensor_mul(r_cols[:, :], a_cols[:, :], cth_cols[:, :]),
                 reads=["a_cols", "cth_cols"], writes=["r_cols"])

            if debug:
                P.dma_out(dbg_h["d_pcols"], p_cols[:, :], reads=["p_cols"])
                P.dma_out(dbg_h["d_rcols"], r_cols[:, :], reads=["r_cols"])
                P.dma_out(dbg_h["d_ga"], ga_tile[:, :], reads=["ga_tile"])

            if upto == "P":
                P.final_join()
            with nc.Block("blk_p") as blk:
                P.flush(blk)
            if upto == "P":
                return P.usage

        with ExitStack() as actx:

            def stage_Pstats(g):
                for tsub in range(2):
                    i = 2 * g + tsub
                    sk = ("ln1", g)
                    P.op("vector", lambda e, i=i, tsub=tsub: e.tensor_reduce(
                            stats[:, tsub:tsub + 1], xbf[:, i * 1024:(i + 1) * 1024],
                            AX.X, ALU.add),
                         reads=[("xbf", i), "stats_a"],
                         writes=[(sk, "s%d" % tsub), "stats_a"])
                    P.op("scalar", lambda e, i=i, tsub=tsub: e.activation(
                            sqscr[:, :], xbf[:, i * 1024:(i + 1) * 1024], AF.Square,
                            accum_out=stats[:, 2 + tsub:3 + tsub]),
                         reads=[("xbf", i), "stats_a", "sqscr"],
                         writes=[(sk, "q%d" % tsub), "sqscr", "stats_a"])
                    if tsub == 1:
                        ln_chain(sk, 1e-5)
                for tsub in range(2):
                    i = 2 * g + tsub
                    P.op("vector", lambda e, i=i, tsub=tsub: e.tensor_scalar(
                            hpre[:, tsub * 1024:(tsub + 1) * 1024],
                            xbf[:, i * 1024:(i + 1) * 1024],
                            stats[:, 4 + tsub:5 + tsub], stats[:, 6 + tsub:7 + tsub],
                            ALU.subtract, ALU.mult),
                         reads=[("xbf", i), (sk, "y"), "stats_a", ("hpre", tsub)],
                         writes=[("hpre", tsub)])

            def stage_Pmm(g):
                sl = g % 2
                h1keys = [("h1", dk) for dk in range(8)]
                sk = ("ln1", g)
                for tsub in range(2):
                    i = 2 * g + tsub
                    hs = tsub * 1024
                    hk = ("hpre", tsub)
                    for dk in range(8):
                        pb = dk % 2
                        P.op("tensor", lambda e, dk=dk, pb=pb, hs=hs: e.matmul(
                                psum[pb][:, 0:128],
                                hpre[:, hs + dk * 128:hs + (dk + 1) * 128],
                                id_bf[:, :], start=True, stop=True),
                             reads=[hk, "id_bf"], writes=[("pb", pb)])
                        P.op("vector", lambda e, tsub=tsub, dk=dk, pb=pb: e.tensor_scalar(
                                h1r[:, dk * 256 + tsub * 128:
                                     dk * 256 + (tsub + 1) * 128],
                                psum[pb][:, 0:128],
                                ca1_cols[:, dk:dk + 1], sa_cols[:, dk:dk + 1],
                                ALU.mult, ALU.add),
                             reads=[("pb", pb), "ca1_cols", "sa_cols"],
                             writes=[("h1", dk)])

                for wnm, dstbuf, dkey, scaled in [("wq", qtr, "qt", True),
                                                  ("wk", ktr, "kt", False)]:
                    for quad in range(2):
                        for hh in range(4):
                            h = quad * 4 + hh
                            pb = (0 if quad == 0 else 2) + hh // 2
                            r0 = (hh % 2) * 256
                            for dk in range(8):
                                P.op("tensor", lambda e, wnm=wnm, h=h, dk=dk, pb=pb,
                                        r0=r0: e.matmul(
                                        psum[pb][:, r0:r0 + 256],
                                        wbf[wnm][:, dk * 1024 + h * 128:
                                                 dk * 1024 + (h + 1) * 128],
                                        h1r[:, dk * 256:(dk + 1) * 256],
                                        start=(dk == 0), stop=(dk == 7)),
                                     reads=[(wnm, dk)] + h1keys, writes=[("pb", pb)])
                        for bi in range(2):
                            pb = (0 if quad == 0 else 2) + bi
                            col = quad * 1024 + bi * 512
                            sct = qscale_t if scaled else kscale_t
                            sck = "qscale_t" if scaled else "kscale_t"
                            P.op("vector", lambda e, pb=pb, col=col, sl=sl, dstbuf=dstbuf,
                                    sct=sct: e.tensor_mul(
                                    dstbuf[:, sl * 2048 + col:sl * 2048 + col + 512],
                                    psum[pb][:, 0:512], sct[:, col:col + 512]),
                                 reads=[("pb", pb), sck],
                                 writes=[(dkey, sl, quad * 2 + bi)])

                for wnm, dstbuf, dkey, actf in [("wv", vtr, "vt", AF.Copy),
                                                ("wg", gtr, "gt", AF.Silu)]:
                    for tsub in range(2):
                        pbs = (0, 1) if tsub == 0 else (2, 3)
                        for half in range(2):
                            pb = pbs[half]
                            for dk in range(8):
                                P.op("tensor", lambda e, wnm=wnm, dk=dk, pb=pb, half=half,
                                        tsub=tsub: e.matmul(
                                        psum[pb][:, 0:512],
                                        h1r[:, dk * 256 + tsub * 128:
                                             dk * 256 + (tsub + 1) * 128],
                                        wbf[wnm][:, dk * 1024 + half * 512:
                                                dk * 1024 + (half + 1) * 512],
                                        start=(dk == 0), stop=(dk == 7)),
                                     reads=[(wnm, dk)] + h1keys, writes=[("pb", pb)])
                            P.op("scalar", lambda e, pb=pb, half=half, tsub=tsub,
                                    dstbuf=dstbuf, actf=actf: e.activation(
                                    dstbuf[:, tsub * 1024 + half * 512:
                                           tsub * 1024 + (half + 1) * 512],
                                    psum[pb][:, 0:512], actf),
                                 reads=[("pb", pb)],
                                 writes=[(dkey, 0, tsub * 2 + half)])

                if debug:
                    for dk in range(8):
                        P.dma_out(dbg_h["d_h1"][dk * 128:(dk + 1) * 128,
                                                g * 256:(g + 1) * 256],
                                  h1r[:, dk * 256:(dk + 1) * 256],
                                  reads=[("h1", dk)])
                    for h in range(H):
                        P.dma_out(dbg_h["d_qt"][h * 128:(h + 1) * 128, g * 256:(g + 1) * 256],
                                  qtr[:, sl * 2048 + h * 256:sl * 2048 + (h + 1) * 256],
                                  reads=[("qt", sl, j) for j in range(4)])
                        P.dma_out(dbg_h["d_kt"][h * 128:(h + 1) * 128, g * 256:(g + 1) * 256],
                                  ktr[:, sl * 2048 + h * 256:sl * 2048 + (h + 1) * 256],
                                  reads=[("kt", sl, j) for j in range(4)])
                    for tsub in range(2):
                        i = 2 * g + tsub
                        P.dma_out(dbg_h["d_v"][i * 128:(i + 1) * 128, :],
                                  vtr[:, tsub * 1024:(tsub + 1) * 1024],
                                  reads=[("vt", 0, tsub * 2), ("vt", 0, tsub * 2 + 1)])
                        P.dma_out(dbg_h["d_g"][i * 128:(i + 1) * 128, :],
                                  gtr[:, tsub * 1024:(tsub + 1) * 1024],
                                  reads=[("gt", 0, tsub * 2), ("gt", 0, tsub * 2 + 1)])

            def stage_R(g):
                sl = g % 2
                qkeys = [("qt", sl, j) for j in range(4)]
                kkeys = [("kt", sl, j) for j in range(4)]
                for tsub in range(2):
                    i = 2 * g + tsub
                    mv = (i > 0)
                    vkeys = [("vt", 0, tsub * 2), ("vt", 0, tsub * 2 + 1)]
                    for h in range(H):
                        pb = 6 + h // 4
                        P.op("tensor", lambda e, h=h, pb=pb, sl=sl, tsub=tsub: e.matmul(
                                psum[pb][:, (h % 4) * 128:(h % 4 + 1) * 128],
                                ktr[:, sl * 2048 + h * 256 + tsub * 128:
                                     sl * 2048 + h * 256 + (tsub + 1) * 128],
                                id_bf[:, :], start=True, stop=True),
                             reads=kkeys + ["id_bf"], writes=[("pb", pb)])
                    for bi in range(2):
                        pb = 6 + bi
                        t_copy("vector",
                               ktm[:, bi * 512:(bi + 1) * 512],
                               psum[pb][:, 0:512], [("pb", pb)], [("ktm", bi)])
                    for h in range(H):
                        pb = 6 + h // 4
                        P.op("tensor", lambda e, h=h, pb=pb, sl=sl, tsub=tsub: e.matmul(
                                psum[pb][:, (h % 4) * 128:(h % 4 + 1) * 128],
                                ktr[:, sl * 2048 + h * 256 + tsub * 128:
                                     sl * 2048 + h * 256 + (tsub + 1) * 128],
                                qtr[:, sl * 2048 + h * 256 + tsub * 128:
                                     sl * 2048 + h * 256 + (tsub + 1) * 128],
                                start=True, stop=True),
                             reads=kkeys + qkeys, writes=[("pb", pb)])
                    for bi in range(2):
                        pb = 6 + bi
                        P.op("vector", lambda e, pb=pb, bi=bi: e.tensor_mul(
                                abuf[:, bi * 512:(bi + 1) * 512],
                                psum[pb][:, 0:512], mask4_t[:, :]),
                             reads=[("pb", pb), "mask4_t"], writes=[("ab", bi)])
                    for h in range(H):
                        pb = 4 + h // 4
                        P.op("tensor", lambda e, h=h, pb=pb, tsub=tsub,
                                mv=mv: e.matmul(
                                psum[pb][:, (h % 4) * 128:(h % 4 + 1) * 128],
                                abuf[:, h * 128:(h + 1) * 128],
                                vtr[:, tsub * 1024 + h * 128:
                                     tsub * 1024 + (h + 1) * 128],
                                start=True, stop=not mv),
                             reads=[("ab", h // 4)] + vkeys, writes=[("pb", pb)])
                        if mv:
                            P.op("tensor", lambda e, h=h, pb=pb, sl=sl, tsub=tsub: e.matmul(
                                    psum[pb][:, (h % 4) * 128:(h % 4 + 1) * 128],
                                    qtr[:, sl * 2048 + h * 256 + tsub * 128:
                                         sl * 2048 + h * 256 + (tsub + 1) * 128],
                                    Mbf[:, h * 128:(h + 1) * 128],
                                    start=False, stop=True),
                                 reads=qkeys + [("Mbf", h)], writes=[("pb", pb)])
                    for h in range(H):
                        pb = 6 + h // 4
                        P.op("tensor", lambda e, h=h, pb=pb, tsub=tsub: e.matmul(
                                psum[pb][:, (h % 4) * 128:(h % 4 + 1) * 128],
                                ktm[:, h * 128:(h + 1) * 128],
                                vtr[:, tsub * 1024 + h * 128:
                                     tsub * 1024 + (h + 1) * 128],
                                start=True, stop=True),
                             reads=[("ktm", h // 4)] + vkeys, writes=[("pb", pb)])
                    for h in range(H):
                        pb = 6 + h // 4
                        if mv:
                            P.op("vector", lambda e, h=h, pb=pb: e.scalar_tensor_tensor(
                                    Mst[:, h * 128:(h + 1) * 128],
                                    Mst[:, h * 128:(h + 1) * 128], GAMMA_C[h],
                                    psum[pb][:, (h % 4) * 128:(h % 4 + 1) * 128],
                                    ALU.mult, ALU.add),
                                 reads=[("pb", pb), ("Mst", h)], writes=[("Mst", h)])
                        else:
                            P.op("vector", lambda e, h=h, pb=pb: e.tensor_copy(
                                    Mst[:, h * 128:(h + 1) * 128],
                                    psum[pb][:, (h % 4) * 128:(h % 4 + 1) * 128]),
                                 reads=[("pb", pb)], writes=[("Mst", h)])
                        P.op("scalar", lambda e, h=h: e.activation(
                                Mbf[:, h * 128:(h + 1) * 128],
                                Mst[:, h * 128:(h + 1) * 128],
                                AF.Identity, scale=GAMMA_C[h]),
                             reads=[("Mst", h)], writes=[("Mbf", h)])
                    gk = ("gn", i)
                    for bi in range(2):
                        pb = 4 + bi
                        P.op("vector", lambda e, pb=pb, bi=bi: e.tensor_reduce(
                                stats[:, 8 + bi * 4:8 + (bi + 1) * 4],
                                psum[pb][:, 0:512].rearrange("p (h x) -> p h x", x=128),
                                AX.X, ALU.add),
                             reads=[("pb", pb), "stats_b"],
                             writes=[(gk, "s"), "stats_b"])
                        P.op("scalar", lambda e, pb=pb, bi=bi: e.activation(
                                sqscr[:, bi * 512:(bi + 1) * 512], psum[pb][:, 0:512],
                                AF.Square),
                             reads=[("pb", pb), "sqscr"], writes=["sqscr"])
                        P.op("vector", lambda e, bi=bi: e.tensor_reduce(
                                stats[:, 16 + bi * 4:16 + (bi + 1) * 4],
                                sqscr[:, bi * 512:(bi + 1) * 512]
                                .rearrange("p (h x) -> p h x", x=128),
                                AX.X, ALU.add),
                             reads=["sqscr", "stats_b"], writes=[(gk, "q"), "stats_b"])
                    P.op("vector", lambda e: e.tensor_scalar(
                            stats[:, 8:16], stats[:, 8:16], 1.0 / DH, None, ALU.mult),
                         reads=[(gk, "s"), "stats_b"], writes=[(gk, "mu"), "stats_b"])
                    P.op("vector", lambda e: e.tensor_scalar(
                            stats[:, 16:24], stats[:, 16:24], 1.0 / DH, None, ALU.mult),
                         reads=[(gk, "q"), "stats_b"], writes=[(gk, "msq"), "stats_b"])
                    P.op("vector", lambda e: e.scalar_tensor_tensor(
                            stats[:, 40:48], stats[:, 8:16], -1.0, stats[:, 8:16],
                            ALU.mult, ALU.mult),
                         reads=[(gk, "mu"), "stats_b"], writes=[(gk, "m2"), "stats_b"])
                    P.op("vector", lambda e: e.tensor_add(
                            stats[:, 40:48], stats[:, 40:48], stats[:, 16:24]),
                         reads=[(gk, "m2"), (gk, "msq"), "stats_b"],
                         writes=[(gk, "var"), "stats_b"])
                    P.op("vector", lambda e: e.tensor_scalar(
                            stats[:, 40:48], stats[:, 40:48], 1e-6, None, ALU.add),
                         reads=[(gk, "var"), "stats_b"], writes=[(gk, "veps"), "stats_b"])
                    rsqrt_cols(gk, "stats_b", (40, 48), (16, 24), (24, 32))
                    P.op("vector", lambda e: e.scalar_tensor_tensor(
                            stats[:, 8:16], stats[:, 8:16], -1.0, stats[:, 16:24],
                            ALU.mult, ALU.mult),
                         reads=[(gk, "mu"), (gk, "y"), "stats_b"],
                         writes=[(gk, "mrs"), "stats_b"])
                    for h in range(H):
                        pb = 4 + h // 4
                        P.op("scalar", lambda e, h=h, pb=pb: e.activation(
                                retn[:, h * 128:(h + 1) * 128],
                                psum[pb][:, (h % 4) * 128:(h % 4 + 1) * 128], AF.Identity,
                                bias=stats[:, 8 + h:9 + h], scale=stats[:, 16 + h:17 + h]),
                             reads=[("pb", pb), (gk, "mrs"), (gk, "y"), "stats_b"],
                             writes=["retn"])
                    if debug:
                        P.dma_out(dbg_h["d_ret"][i * 128:(i + 1) * 128, :],
                                  retn[:, :], reads=["retn"])
                        P.dma_out(dbg_h["d_mbf"][i * 128:(i + 1) * 128, :],
                                  Mbf[:, :], reads=[("Mbf", h) for h in range(H)])
                    P.op("vector", lambda e, tsub=tsub: e.tensor_mul(
                            gated[:, tsub * 1024:(tsub + 1) * 1024],
                            retn[:, 0:1024],
                            gtr[:, tsub * 1024:(tsub + 1) * 1024]),
                         reads=["retn", ("gt", 0, tsub * 2), ("gt", 0, tsub * 2 + 1)],
                         writes=[("gated", tsub)])

            def stage_W(g):
                for tsub in range(2):
                    i = 2 * g + tsub
                    for dk in range(8):
                        pb = 2 + (dk % 2)
                        rg = ((dk // 2) % 4)
                        P.op("tensor", lambda e, dk=dk, pb=pb, rg=rg, tsub=tsub: e.matmul(
                                psum[pb][:, rg * 128:(rg + 1) * 128],
                                gated[:, tsub * 1024 + dk * 128:
                                      tsub * 1024 + (dk + 1) * 128],
                                id_bf[:, :], start=True, stop=True),
                             reads=[("gated", tsub), "id_bf"], writes=[("pb", pb)])
                        P.op("vector", lambda e, dk=dk, pb=pb, rg=rg: e.tensor_copy(
                                gfm[:, dk * 128:(dk + 1) * 128],
                                psum[pb][:, rg * 128:(rg + 1) * 128]),
                             reads=[("pb", pb)], writes=["gfm"])
                    for half in range(2):
                        pb = half
                        for dk in range(8):
                            P.op("tensor", lambda e, dk=dk, pb=pb, half=half: e.matmul(
                                    psum[pb][:, 0:512],
                                    gfm[:, dk * 128:(dk + 1) * 128],
                                    wbf["wo"][:, dk * 1024 + half * 512:
                                              dk * 1024 + (half + 1) * 512],
                                    start=(dk == 0), stop=(dk == 7)),
                                 reads=["gfm", ("wo", dk, half)], writes=[("pb", pb)])
                        P.op("vector", lambda e, pb=pb, half=half, i=i: e.tensor_add(
                                xbf[:, i * 1024 + half * 512:i * 1024 + (half + 1) * 512],
                                psum[pb][:, 0:512],
                                xbf[:, i * 1024 + half * 512:i * 1024 + (half + 1) * 512]),
                             reads=[("pb", pb), ("xbf", i)], writes=[("xbf", i)])
                    if debug:
                        P.dma_out(dbg_h["d_x2"][i * 128:(i + 1) * 128, :],
                                  xbf[:, i * 1024:(i + 1) * 1024],
                                  reads=[("xbf", i)])

            if upto.startswith(("A", "R")):
                stage_Pstats(0)
                stage_Pmm(0)
                P.final_join()
                with nc.Block("blk_a") as blk:
                    P.flush(blk)
                return P.usage

            mod_pieces = {0: [0], 1: [1, 2], 2: [3], 3: [4], 4: [5], 5: [6, 7]}
            stage_Pstats(0)
            for g in range(NG):
                if g >= 2:
                    stage_W(g - 2)
                if g >= 1:
                    stage_R(g - 1)
                stage_Pmm(g)
                if g + 1 < NG:
                    stage_Pstats(g + 1)
                if g in mod_pieces:
                    emit_matvec_half(1, dks=mod_pieces[g])
            stage_W(NG - 2)
            stage_R(NG - 1)
            stage_W(NG - 1)
            emit_mod_right_final()

            with nc.Block("blk_a") as blk:
                P.flush(blk)
        ab_ctx.close()
        P.fence()

        with ExitStack() as bctx:
            w1bf = alloc(bctx, "w1bf", 128, 8 * 4096, BF16)
            w2bf = alloc(bctx, "w2bf", 128, 32 * 1024, BF16)
            h2r = alloc(bctx, "h2r", 128, 2 * 2048, BF16)
            h2pre = alloc(bctx, "h2pre", 128, 1024, BF16)
            actb = alloc(bctx, "actb", 128, 4 * 256, BF16)
            csin = alloc(bctx, "csin", 128, 4 * 512, BF16)
            t1b = alloc(bctx, "t1b", 128, 4 * 256)
            outr = alloc(bctx, "outr", 128, 1024)

            for qt in range(4):
                for dk in range(8):
                    P.dma_cast(w1bf[:, dk * 4096 + qt * 1024:dk * 4096 + (qt + 1) * 1024],
                               W1_h[dk * 128:(dk + 1) * 128, qt * 1024:(qt + 1) * 1024],
                               writes=[("w1g", qt)], chain=(dk == 0), sem_idx=0,
                               force_inc=True)
                for fk8 in range(8):
                    fk = qt * 8 + fk8
                    P.dma_cast(w2bf[:, fk * 1024:(fk + 1) * 1024],
                               W2_h[fk * 128:(fk + 1) * 128, :],
                               writes=[("w2g", qt)], chain=(fk8 == 0), sem_idx=1,
                               force_inc=True)

            def ln2_stats(tau):
                for tsub in range(2):
                    i = 2 * tau + tsub
                    sk = ("ln2", tau)
                    P.op("vector", lambda e, i=i, tsub=tsub: e.tensor_reduce(
                            stats[:, tsub:tsub + 1], xbf[:, i * 1024:(i + 1) * 1024],
                            AX.X, ALU.add),
                         reads=[("xbf", i), "stats_a"],
                         writes=[(sk, "s%d" % tsub), "stats_a"])
                    P.op("scalar", lambda e, i=i, tsub=tsub: e.activation(
                            sqscr[:, :], xbf[:, i * 1024:(i + 1) * 1024], AF.Square,
                            accum_out=stats[:, 2 + tsub:3 + tsub]),
                         reads=[("xbf", i), "stats_a", "sqscr"],
                         writes=[(sk, "q%d" % tsub), "sqscr", "stats_a"])
                    if tsub == 1:
                        ln_chain(sk, 1e-5)

            def ln2_tr(tau):
                sl = tau % 2
                for tsub in range(2):
                    i = 2 * tau + tsub
                    sk = ("ln2", tau)
                    P.op("vector", lambda e, i=i, tsub=tsub: e.tensor_scalar(
                            h2pre[:, :], xbf[:, i * 1024:(i + 1) * 1024],
                            stats[:, 4 + tsub:5 + tsub], stats[:, 6 + tsub:7 + tsub],
                            ALU.subtract, ALU.mult),
                         reads=[("xbf", i), (sk, "y"), "stats_a", "h2pre"],
                         writes=["h2pre"])
                    for dk in range(8):
                        pb = 6 + (dk % 2)
                        P.op("tensor", lambda e, dk=dk, pb=pb: e.matmul(
                                psum[pb][:, 0:128],
                                h2pre[:, dk * 128:(dk + 1) * 128],
                                id_bf[:, :], start=True, stop=True),
                             reads=["h2pre", "id_bf"], writes=[("pb", pb)])
                        P.op("vector", lambda e, sl=sl, tsub=tsub, dk=dk, pb=pb:
                                e.tensor_scalar(
                                h2r[:, sl * 2048 + dk * 256 + tsub * 128:
                                     sl * 2048 + dk * 256 + (tsub + 1) * 128],
                                psum[pb][:, 0:128],
                                cb1_cols[:, dk:dk + 1], sb_cols[:, dk:dk + 1],
                                ALU.mult, ALU.add),
                             reads=[("pb", pb), "cb1_cols", "sb_cols"],
                             writes=[("h2", sl, dk)])
                if debug:
                    for dk in range(8):
                        P.dma_out(dbg_h["d_h2"][dk * 128:(dk + 1) * 128,
                                                tau * 256:(tau + 1) * 256],
                                  h2r[:, sl * 2048 + dk * 256:sl * 2048 + (dk + 1) * 256],
                                  reads=[("h2", sl, dk)])

            for tau in range(NTT):
                sl = tau % 2
                h2keys = [("h2", sl, dk) for dk in range(8)]
                if tau == 0:
                    ln2_stats(0)
                    ln2_tr(0)

                def w1_mm(fk):
                    pbh = fk % 2
                    for dk in range(8):
                        P.op("tensor", lambda e, fk=fk, dk=dk, pbh=pbh, sl=sl: e.matmul(
                                psum[pbh][:, 0:256],
                                w1bf[:, dk * 4096 + fk * 128:dk * 4096 + (fk + 1) * 128],
                                h2r[:, sl * 2048 + dk * 256:sl * 2048 + (dk + 1) * 256],
                                start=(dk == 0), stop=(dk == 7)),
                             reads=[("w1g", fk // 8)] + h2keys, writes=[("pb", pbh)])

                def mid(fk):
                    pbh = fk % 2
                    cs = fk % 4
                    P.op("scalar", lambda e, pbh=pbh, cs=cs: e.activation(
                            csin[:, cs * 512:cs * 512 + 256], psum[pbh][:, 0:256], AF.Sin,
                            bias=halfpi_col[:, 0:1], scale=tenth_col[:, 0:1]),
                         reads=[("pb", pbh), "halfpi_col", "tenth_col"],
                         writes=[("csin", cs)])
                    P.op("scalar", lambda e, pbh=pbh, cs=cs: e.activation(
                            csin[:, cs * 512 + 256:cs * 512 + 512], psum[pbh][:, 0:256],
                            AF.Sin, scale=tenth_col[:, 0:1]),
                         reads=[("pb", pbh), "tenth_col"], writes=[("ssin", cs)])
                    P.op("vector", lambda e, pbh=pbh, cs=cs, fk=fk: e.scalar_tensor_tensor(
                            t1b[:, cs * 256:(cs + 1) * 256],
                            csin[:, cs * 512:cs * 512 + 256], p_cols[:, fk:fk + 1],
                            psum[pbh][:, 0:256], ALU.mult, ALU.add),
                         reads=[("csin", cs), ("pb", pbh), "p_cols"], writes=[("t1", cs)])
                    P.op("vector", lambda e, cs=cs, fk=fk: e.scalar_tensor_tensor(
                            t1b[:, cs * 256:(cs + 1) * 256],
                            csin[:, cs * 512 + 256:cs * 512 + 512], r_cols[:, fk:fk + 1],
                            t1b[:, cs * 256:(cs + 1) * 256], ALU.mult, ALU.add),
                         reads=[("ssin", cs), ("t1", cs), "r_cols"], writes=[("t1", cs)])
                    P.op("scalar", lambda e, cs=cs, fk=fk: e.activation(
                            actb[:, (fk % 4) * 256:(fk % 4 + 1) * 256],
                            t1b[:, cs * 256:(cs + 1) * 256], AF.Relu,
                            bias=b1_cols[:, fk:fk + 1]),
                         reads=[("t1", cs), "b1_cols"], writes=[("act", fk % 4)])

                def w2_mm(fk):
                    for tsub in range(2):
                        for half in range(2):
                            pb = 2 + tsub * 2 + half
                            P.op("tensor", lambda e, fk=fk, tsub=tsub, half=half, pb=pb:
                                    e.matmul(
                                    psum[pb][:, 0:512],
                                    actb[:, (fk % 4) * 256 + tsub * 128:
                                         (fk % 4) * 256 + (tsub + 1) * 128],
                                    w2bf[:, fk * 1024 + half * 512:
                                         fk * 1024 + (half + 1) * 512],
                                    start=(fk == 0), stop=(fk == 31)),
                                 reads=[("act", fk % 4), ("w2g", fk // 8)],
                                 writes=[("pb", pb)])

                for step in range(34):
                    if step >= 2:
                        w2_mm(step - 2)
                    if step < 32:
                        w1_mm(step)
                        mid(step)
                if tau + 1 < NTT:
                    ln2_stats(tau + 1)
                    ln2_tr(tau + 1)
                for tsub in range(2):
                    i = 2 * tau + tsub
                    for half in range(2):
                        pb = 2 + tsub * 2 + half
                        ok = (i * 2 + half) % 2
                        P.op("vector", lambda e, pb=pb, ok=ok, half=half: e.tensor_mul(
                                outr[:, ok * 512:(ok + 1) * 512],
                                psum[pb][:, 0:512],
                                gb_tile[:, half * 512:(half + 1) * 512]),
                             reads=[("pb", pb), "gb_tile"], writes=[("outr", ok)])
                        P.op("vector", lambda e, ok=ok, half=half: e.tensor_add(
                                outr[:, ok * 512:(ok + 1) * 512],
                                outr[:, ok * 512:(ok + 1) * 512],
                                gbb2_tile[:, half * 512:(half + 1) * 512]),
                             reads=["gbb2_tile", ("outr", ok)], writes=[("outr", ok)])
                        P.op("vector", lambda e, ok=ok, half=half, i=i: e.tensor_add(
                                outr[:, ok * 512:(ok + 1) * 512],
                                outr[:, ok * 512:(ok + 1) * 512],
                                xbf[:, i * 1024 + half * 512:i * 1024 + (half + 1) * 512]),
                             reads=[("xbf", i), ("outr", ok)],
                             writes=[("outr", ok)])
                        P.dma_out(out_h[i * 128:(i + 1) * 128, half * 512:(half + 1) * 512],
                                  outr[:, ok * 512:(ok + 1) * 512],
                                  reads=[("outr", ok)])

            P.final_join()
            with nc.Block("blk_b") as blk:
                P.flush(blk)

    return P.usage


_CACHED = {}


def kernel(**inputs):
    nc = _CACHED.get("nc")
    if nc is None:
        nc = build_program(debug=False)
        _CACHED["nc"] = nc
    cst = host_constants()
    X = np.ascontiguousarray(np.asarray(inputs["L0_attn_X"]))
    Cc = np.ascontiguousarray(np.asarray(inputs["L0_c"]))
    base = {}
    for k in ["L0_attn_Wq", "L0_attn_Wk", "L0_attn_Wv", "L0_attn_W_G", "L0_attn_W_O",
              "L0_ffn_W1", "L0_ffn_W2", "L0_wmod"]:
        base[k] = np.ascontiguousarray(np.asarray(inputs[k]))
    for k in ["L0_ffn_b1", "L0_ffn_quantum_phases", "L0_ffn_quantum_weights_real",
              "L0_ffn_quantum_weights_imag"]:
        base[k] = np.ascontiguousarray(np.asarray(inputs[k]).reshape(32, 128))
    base["L0_ffn_b2"] = np.ascontiguousarray(np.asarray(inputs["L0_ffn_b2"]).reshape(8, 128))
    for k, v in cst.items():
        base[k] = np.ascontiguousarray(v)
    in_maps = []
    for b in range(8):
        m = dict(base)
        m["L0_attn_X"] = np.ascontiguousarray(X[b])
        m["L0_c"] = np.ascontiguousarray(Cc[b].reshape(8, 128))
        in_maps.append(m)
    res = run_bass_kernel_spmd(nc, in_maps, core_ids=list(range(8)))
    return np.stack([np.asarray(r["OUT"]) for r in res.results], axis=0)


# version 12 (final)
# speedup vs baseline: 1.7108x; 1.0202x over previous
import numpy as np
import ml_dtypes
from contextlib import ExitStack

import concourse.bass as bass
import concourse.mybir as mybir
from concourse.bass_utils import run_bass_kernel_spmd

F32 = mybir.dt.float32
BF16 = mybir.dt.bfloat16
AF = mybir.ActivationFunctionType
ALU = mybir.AluOpType
AX = mybir.AxisListType

S, D, H, DH = 2048, 1024, 8, 128
C = 128
GRP = 256
NG = S // GRP
DFF = 4096
NTT = S // 256
PI = float(np.pi)

ENGINES = ("sync", "tensor", "scalar", "vector", "gpsimd")


class Tok:
    __slots__ = ("sem_name", "order", "used", "val")

    def __init__(self, sem_name, order):
        self.sem_name = sem_name
        self.order = order
        self.used = False
        self.val = None


class Prog:

    def __init__(self, nc, ctx, usage=None):
        self.nc = nc
        self.phase2 = usage is not None
        self.usage = usage if usage is not None else set()
        if self.phase2:
            self.sems = {e: ctx.enter_context(nc.semaphore(f"sm_{e}")) for e in ENGINES}
            for j in range(4):
                self.sems[f"dma{j}"] = ctx.enter_context(nc.semaphore(f"sm_dma{j}"))
            for j in range(2):
                self.sems[f"odma{j}"] = ctx.enter_context(nc.semaphore(f"sm_odma{j}"))
            for j in range(2):
                self.sems[f"gdma{j}"] = ctx.enter_context(nc.semaphore(f"sm_gdma{j}"))
        else:
            self.sems = {e: None for e in ENGINES}
            for j in range(4):
                self.sems[f"dma{j}"] = None
            for j in range(2):
                self.sems[f"odma{j}"] = None
            for j in range(2):
                self.sems[f"gdma{j}"] = None
        self.cnt = {k: 0 for k in self.sems}
        self.order_ctr = {k: 0 for k in self.sems}
        self.last_tok = {k: None for k in self.sems}
        self.W = {}
        self.R = {}
        self.lists = {e: [] for e in ENGINES}
        self.mark = {e: {k: 0 for k in self.sems} for e in ENGINES}
        self.seen = set()
        self.fence_vals = None
        self.n_incs = 0

    def _wait(self, eng, tok):
        if isinstance(tok, Tok):
            if eng == "tensor" and tok.sem_name == "tensor":
                return
            if self.mark[eng][tok.sem_name] >= tok.order:
                return
            self.mark[eng][tok.sem_name] = tok.order
            if not self.phase2:
                tok.used = True
                self.usage.add((tok.sem_name, tok.order))
                return
            assert tok.used and tok.val is not None, (eng, tok.sem_name, tok.order)
            sem = self.sems[tok.sem_name]
            val = tok.val
            self.lists[eng].append(lambda e, sem=sem, val=val: e.wait_ge(sem, val))
        else:
            sem_name, val = tok
            if not self.phase2 or val <= 0:
                return
            key = ("val", sem_name)
            cur = self.mark[eng].get(key, 0)
            if cur >= val:
                return
            self.mark[eng][key] = val
            sem = self.sems[sem_name]
            self.lists[eng].append(lambda e, sem=sem, val=val: e.wait_ge(sem, val))

    def fence(self):
        for k, t in self.last_tok.items():
            if t is not None and not self.phase2:
                t.used = True
                self.usage.add((t.sem_name, t.order))
        self.fence_vals = {k: self.cnt[k] for k in self.cnt}

    def op(self, eng, fn, reads=(), writes=(), sem=None, inc=1, extra_waits=(),
           force_inc=False):
        reads = list(reads)
        writes = list(writes)
        for t in extra_waits:
            self._wait(eng, t)
        for k in reads + writes:
            if k not in self.seen:
                self.seen.add(k)
                if self.fence_vals is not None:
                    for sname, val in self.fence_vals.items():
                        self._wait(eng, (sname, val))
        for k in reads:
            t = self.W.get(k)
            if t is not None:
                self._wait(eng, t)
            if isinstance(k, tuple) and len(k) == 2 and k[0] == "pb":
                sem_name = sem or eng
                for t in self.R.get(k, ()):
                    if t.sem_name != sem_name:
                        self._wait(eng, t)
        for k in writes:
            t = self.W.get(k)
            if t is not None:
                self._wait(eng, t)
            for t in self.R.get(k, ()):
                self._wait(eng, t)
        sem_name = sem or eng
        self.order_ctr[sem_name] += 1
        tok = Tok(sem_name, self.order_ctr[sem_name])
        self.last_tok[sem_name] = tok
        if force_inc and not self.phase2:
            tok.used = True
            self.usage.add((sem_name, tok.order))
        if self.phase2:
            if (sem_name, tok.order) in self.usage:
                tok.used = True
                self.cnt[sem_name] += inc
                tok.val = self.cnt[sem_name]
                self.n_incs += 1
            semobj = self.sems[sem_name]

            def closure(e, fn=fn, semobj=semobj, inc=inc, tok=tok):
                inst = fn(e)
                if tok.used:
                    inst.then_inc(semobj, inc)

            self.lists[eng].append(closure)
        for k in writes:
            self.W[k] = tok
            self.R[k] = []
        for k in reads:
            self.R.setdefault(k, []).append(tok)
        return tok

    def dma_in(self, dst, src, reads=(), writes=()):
        j = self.order_ctr.get("_dma_rr", 0)
        self.order_ctr["_dma_rr"] = j + 1
        sem_name = f"dma{j % 4}"
        prev = self.last_tok[sem_name]
        return self.op("sync", lambda e: e.dma_start(dst, src),
                       reads=reads, writes=writes, sem=sem_name, inc=16,
                       extra_waits=[prev] if prev is not None else [])

    def dma_cast(self, dst, src, reads=(), writes=(), chain=True, sem_idx=None,
                 force_inc=False):
        if sem_idx is None:
            j = self.order_ctr.get("_gdma_rr", 0)
            self.order_ctr["_gdma_rr"] = j + 1
            sem_name = f"gdma{j % 2}"
        else:
            sem_name = f"gdma{sem_idx}"
        prev = self.last_tok[sem_name]
        return self.op("gpsimd", lambda e: e.dma_start(dst, src),
                       reads=reads, writes=writes, sem=sem_name, inc=16,
                       extra_waits=[prev] if (chain and prev is not None) else [],
                       force_inc=force_inc)

    def dma_out(self, dst, src, reads=(), writes=()):
        j = self.order_ctr.get("_odma_rr", 0)
        self.order_ctr["_odma_rr"] = j + 1
        sem_name = f"odma{j % 2}"
        prev = self.last_tok[sem_name]
        return self.op("scalar", lambda e: e.dma_start(dst, src),
                       reads=reads, writes=writes, sem=sem_name, inc=16,
                       extra_waits=[prev] if prev is not None else [])

    def final_join(self):
        for k, t in self.last_tok.items():
            if t is not None and not self.phase2:
                t.used = True
                self.usage.add((t.sem_name, t.order))
        tok = {k: self.cnt[k] for k in self.cnt if not k.startswith("_")}
        for e in ENGINES:
            for sname, val in tok.items():
                self._wait(e, (sname, val))

    def collect_usage(self):
        return self.usage

    def flush(self, blk):
        if not self.phase2:
            for e in ENGINES:
                self.lists[e] = []
            return
        for e in ENGINES:
            lst = self.lists[e]
            if not lst:
                continue

            def body(engine, lst=lst):
                for fn in lst:
                    fn(engine)

            getattr(blk, e)(body)
            self.lists[e] = []


GAMMAS = 1.0 - np.exp2(-5.0 - np.arange(8, dtype=np.float64))
GAMMA_C = [float(g ** 128) for g in GAMMAS]


def host_constants():
    cst = {}
    cst["ones_row"] = np.ones((1, 128), np.float32)
    m = (np.arange(128)[None, :] >= np.arange(128)[:, None]).astype(np.float32)
    cst["mask4_t"] = np.ascontiguousarray(np.tile(m, (1, 4)))
    cst["id_bf"] = np.eye(128, dtype=np.float32).astype(ml_dtypes.bfloat16)
    cst["id_f32"] = np.eye(128, dtype=np.float32)
    cst["halfpi_col"] = np.full((128, 1), np.pi / 2.0, np.float32)
    cst["tenth_col"] = np.full((128, 1), 0.1, np.float32)
    n = np.arange(128, dtype=np.float64)
    qs = np.empty((1, 2048), np.float64)
    ks = np.empty((1, 2048), np.float64)
    for h in range(8):
        qs[0, h * 256:h * 256 + 128] = GAMMAS[h] ** n / np.sqrt(128.0)
        qs[0, h * 256 + 128:h * 256 + 256] = qs[0, h * 256:h * 256 + 128]
        ks[0, h * 256:h * 256 + 128] = GAMMAS[h] ** (-n)
        ks[0, h * 256 + 128:h * 256 + 256] = ks[0, h * 256:h * 256 + 128]
    cst["qscale_t"] = np.ascontiguousarray(
        np.broadcast_to(qs, (128, 2048))).astype(ml_dtypes.bfloat16)
    cst["kscale_t"] = np.ascontiguousarray(
        np.broadcast_to(ks, (128, 2048))).astype(ml_dtypes.bfloat16)
    return cst


def build_program(debug=False, upto="B"):
    usage = _emit(bass.Bass("TRN2", target_bir_lowering=False), None, debug, upto)
    nc = bass.Bass("TRN2", target_bir_lowering=False)
    _emit(nc, usage, debug, upto)
    return nc


def _emit(nc, usage, debug, upto="B"):

    X_h = nc.dram_tensor("L0_attn_X", [S, D], F32, kind="ExternalInput").ap()
    Wq_h = nc.dram_tensor("L0_attn_Wq", [D, D], F32, kind="ExternalInput").ap()
    Wk_h = nc.dram_tensor("L0_attn_Wk", [D, D], F32, kind="ExternalInput").ap()
    Wv_h = nc.dram_tensor("L0_attn_Wv", [D, D], F32, kind="ExternalInput").ap()
    Wg_h = nc.dram_tensor("L0_attn_W_G", [D, D], F32, kind="ExternalInput").ap()
    Wo_h = nc.dram_tensor("L0_attn_W_O", [D, D], F32, kind="ExternalInput").ap()
    W1_h = nc.dram_tensor("L0_ffn_W1", [D, DFF], F32, kind="ExternalInput").ap()
    W2_h = nc.dram_tensor("L0_ffn_W2", [DFF, D], F32, kind="ExternalInput").ap()
    b1_h = nc.dram_tensor("L0_ffn_b1", [32, 128], F32, kind="ExternalInput").ap()
    th_h = nc.dram_tensor("L0_ffn_quantum_phases", [32, 128], F32, kind="ExternalInput").ap()
    wr_h = nc.dram_tensor("L0_ffn_quantum_weights_real", [32, 128], F32, kind="ExternalInput").ap()
    wi_h = nc.dram_tensor("L0_ffn_quantum_weights_imag", [32, 128], F32, kind="ExternalInput").ap()
    b2_h = nc.dram_tensor("L0_ffn_b2", [8, 128], F32, kind="ExternalInput").ap()
    c_h = nc.dram_tensor("L0_c", [8, 128], F32, kind="ExternalInput").ap()
    wm_h = nc.dram_tensor("L0_wmod", [D, 6 * D], F32, kind="ExternalInput").ap()
    cst_h = {}
    for name, arr in host_constants().items():
        dt = BF16 if arr.dtype == ml_dtypes.bfloat16 else F32
        cst_h[name] = nc.dram_tensor(name, list(arr.shape), dt, kind="ExternalInput").ap()
    out_h = nc.dram_tensor("OUT", [S, D], F32, kind="ExternalOutput").ap()
    dbg_h = {}
    if debug:
        for nm, shp, dt_ in [("d_silu", [128, 8], F32), ("d_modc", [128, 48], F32),
                             ("d_ga", [128, D], F32), ("d_ret", [S, D], BF16),
                             ("d_mbf", [S, D], BF16), ("d_gated", [S, D], BF16),
                             ("d_h1", [D, S], BF16),
                             ("d_qt", [D, S], BF16), ("d_kt", [D, S], BF16),
                             ("d_v", [S, D], BF16), ("d_g", [S, D], BF16),
                             ("d_x2", [S, D], BF16), ("d_h2", [D, S], BF16),
                             ("d_pcols", [128, 32], F32), ("d_rcols", [128, 32], F32)]:
            dbg_h[nm] = nc.dram_tensor(nm, shp, dt_, kind="ExternalOutput").ap()

    with ExitStack() as ctx:
        P = Prog(nc, ctx, usage)

        def alloc(c_, name, pdim, fdim, dtype=F32):
            return c_.enter_context(nc.sbuf_tensor(name, [pdim, fdim], dtype))

        xbf = alloc(ctx, "xbf", 128, 16 * 1024, BF16)
        ones_row = alloc(ctx, "sb_ones_row", 1, 128)
        id_bf = alloc(ctx, "sb_id_bf", 128, 128, BF16)
        halfpi_col = alloc(ctx, "sb_halfpi_col", 128, 1)
        tenth_col = alloc(ctx, "sb_tenth_col", 128, 1)
        ca1_cols = alloc(ctx, "ca1_cols", 128, 8)
        sa_cols = alloc(ctx, "sa_cols", 128, 8)
        cb1_cols = alloc(ctx, "cb1_cols", 128, 8)
        sb_cols = alloc(ctx, "sb_cols", 128, 8)
        gb_tile = alloc(ctx, "gb_tile", 128, 1024)
        gbb2_tile = alloc(ctx, "gbb2_tile", 128, 1024)
        p_cols = alloc(ctx, "p_cols", 128, 32)
        r_cols = alloc(ctx, "r_cols", 128, 32)
        b1_cols = alloc(ctx, "b1_cols", 128, 32)
        wstage = alloc(ctx, "wstage", 128, 1024)
        stats = alloc(ctx, "stats", 128, 48)
        sqscr = alloc(ctx, "sqscr", 128, 1024)

        psum = [ctx.enter_context(nc.psum_tensor(f"pb{i}", [128, 512], F32))
                for i in range(8)]

        ab_ctx = ctx.enter_context(ExitStack())
        mask4_t = alloc(ab_ctx, "sb_mask4_t", 128, 512)
        id_f32 = alloc(ab_ctx, "sb_id_f32", 128, 128)
        ga_tile = alloc(ab_ctx, "ga_tile", 128, 1024)
        qscale_t = alloc(ab_ctx, "sb_qscale_t", 128, 2048, BF16)
        kscale_t = alloc(ab_ctx, "sb_kscale_t", 128, 2048, BF16)
        wbf = {}
        for nm in ["wq", "wk", "wv", "wg", "wo"]:
            wbf[nm] = alloc(ab_ctx, nm + "_bf", 128, 8 * 1024, BF16)
        h1r = alloc(ab_ctx, "h1r", 128, 2048, BF16)
        qtr = alloc(ab_ctx, "qtr", 128, 2 * 2048, BF16)
        ktr = alloc(ab_ctx, "ktr", 128, 2 * 2048, BF16)
        vtr = alloc(ab_ctx, "vtr", 128, 2048, BF16)
        gtr = alloc(ab_ctx, "gtr", 128, 2048, BF16)
        gated = alloc(ab_ctx, "gated", 128, 2048, BF16)
        gfm = alloc(ab_ctx, "gfm", 128, 1024, BF16)
        retn = alloc(ab_ctx, "retn", 128, 1024, BF16)
        ktm = alloc(ab_ctx, "ktm", 128, 1024, BF16)
        abuf = alloc(ab_ctx, "abuf", 128, 1024, BF16)
        Mst = alloc(ab_ctx, "Mst", 128, 1024)
        Mbf = alloc(ab_ctx, "Mbf", 128, 1024, BF16)
        hpre = alloc(ab_ctx, "hpre", 128, 2048, BF16)

        def t_copy(eng, dst, src, reads, writes):
            P.op(eng, lambda e: e.tensor_copy(dst, src), reads=reads, writes=writes)

        def rsqrt_cols(eng_key, region, veps_sl, y_sl, u_sl):
            P.op("scalar", lambda e: e.activation(
                    stats[:, u_sl[0]:u_sl[1]], stats[:, veps_sl[0]:veps_sl[1]],
                    AF.Sqrt),
                 reads=[(eng_key, "veps"), region], writes=[(eng_key, "u"), region])
            P.op("vector", lambda e: e.reciprocal(
                    stats[:, y_sl[0]:y_sl[1]], stats[:, u_sl[0]:u_sl[1]]),
                 reads=[(eng_key, "u"), region], writes=[(eng_key, "y"), region])

        def ln_chain(sk, eps):
            P.op("vector", lambda e: e.tensor_scalar(
                    stats[:, 4:6], stats[:, 0:2], 1.0 / D, None, ALU.mult),
                 reads=[(sk, "s0"), (sk, "s1"), "stats_a"],
                 writes=[(sk, "mu"), "stats_a"])
            P.op("vector", lambda e: e.tensor_scalar(
                    stats[:, 2:4], stats[:, 2:4], 1.0 / D, None, ALU.mult),
                 reads=[(sk, "q0"), (sk, "q1"), "stats_a"],
                 writes=[(sk, "msq"), "stats_a"])
            P.op("vector", lambda e: e.scalar_tensor_tensor(
                    stats[:, 6:8], stats[:, 4:6], -1.0, stats[:, 4:6],
                    ALU.mult, ALU.mult),
                 reads=[(sk, "mu"), "stats_a"], writes=[(sk, "m2"), "stats_a"])
            P.op("vector", lambda e: e.tensor_add(
                    stats[:, 2:4], stats[:, 6:8], stats[:, 2:4]),
                 reads=[(sk, "m2"), (sk, "msq"), "stats_a"],
                 writes=[(sk, "var"), "stats_a"])
            P.op("vector", lambda e: e.tensor_scalar(
                    stats[:, 2:4], stats[:, 2:4], eps, None, ALU.add),
                 reads=[(sk, "var"), "stats_a"], writes=[(sk, "veps"), "stats_a"])
            rsqrt_cols(sk, "stats_a", (2, 4), (6, 8), (0, 2))

        silu_cols = alloc(ab_ctx, "silu_cols", 128, 8)
        wmp = alloc(ab_ctx, "wmp", 128, 2 * 768)
        macc = alloc(ab_ctx, "macc", 128, 48)
        ga_cols = alloc(ab_ctx, "ga_cols", 128, 8)
        gb_cols = alloc(ab_ctx, "gb_cols", 128, 8)
        b2_cols = alloc(ab_ctx, "b2_cols", 128, 8)
        ctmp = alloc(ab_ctx, "ctmp", 128, 8)
        ctmp2 = alloc(ab_ctx, "ctmp2", 128, 8)

        def clamp_cols(dst, dst_key, c0):
            mk = ("macc", c0 // 24)
            P.op("vector", lambda e: e.tensor_scalar(
                    ctmp[:, :], macc[:, c0:c0 + 8], 0.0, None, ALU.is_ge),
                 reads=[mk], writes=["ctmp"])
            P.op("vector", lambda e: e.tensor_scalar(
                    ctmp2[:, :], macc[:, c0:c0 + 8], 0.25, None, ALU.max),
                 reads=[mk], writes=["ctmp2"])
            P.op("vector", lambda e: e.tensor_scalar(
                    dst[:, :], macc[:, c0:c0 + 8], -0.25, None, ALU.min),
                 reads=[mk], writes=[dst_key])
            P.op("vector", lambda e: e.tensor_sub(ctmp2[:, :], ctmp2[:, :], dst[:, :]),
                 reads=["ctmp2", dst_key], writes=["ctmp2"])
            P.op("vector", lambda e: e.tensor_mul(ctmp[:, :], ctmp[:, :], ctmp2[:, :]),
                 reads=["ctmp", "ctmp2"], writes=["ctmp"])
            P.op("vector", lambda e: e.tensor_add(dst[:, :], dst[:, :], ctmp[:, :]),
                 reads=["ctmp", dst_key], writes=[dst_key])

        def cols_to_tile(src, src_key, tile, tile_key):
            for rnd in range(2):
                for j in range(4):
                    jj = rnd * 4 + j
                    P.op("tensor", lambda e, jj=jj, j=j: e.matmul(
                            psum[5][0:1, j * 128:(j + 1) * 128],
                            src[:, jj:jj + 1], id_f32[:, :], start=True, stop=True),
                         reads=[src_key, "id_f32"], writes=[("pb", 5)])
                t_copy("vector", tile[0:1, rnd * 512:(rnd + 1) * 512],
                       psum[5][0:1, 0:512], [("pb", 5)], [tile_key])
            for half in (1, 0):
                P.op("tensor", lambda e, half=half: e.matmul(
                        psum[5][:, 0:512], ones_row[:, :],
                        tile[0:1, half * 512:(half + 1) * 512], start=True, stop=True),
                     reads=[tile_key, "ones_row"], writes=[("pb", 5)])
                t_copy("vector", tile[:, half * 512:(half + 1) * 512],
                       psum[5][:, 0:512], [("pb", 5)], [tile_key])

        def emit_matvec_half(half, dks=range(8)):
            for dk in dks:
                for part in range(4):
                    s = (dk * 4 + part) % 2
                    c0h = half * 3072 + part * 768
                    P.dma_in(wmp[:, s * 768:(s + 1) * 768],
                             wm_h[dk * 128:(dk + 1) * 128, c0h:c0h + 768],
                             writes=[("wmp", s)])
                    for c in range(6):
                        ci = part * 6 + c
                        P.op("tensor", lambda e, s=s, c=c, ci=ci, dk=dk: e.matmul(
                                psum[6][:, ci:ci + 1],
                                wmp[:, s * 768 + c * 128:s * 768 + (c + 1) * 128],
                                silu_cols[:, dk:dk + 1],
                                start=True, stop=True),
                             reads=[("wmp", s), "silu_cols"], writes=[("pb", 6)])
                mk = ("macc", half)
                if dk == 0:
                    t_copy("vector", macc[:, half * 24:half * 24 + 24],
                           psum[6][:, 0:24], [("pb", 6)], [mk])
                else:
                    P.op("vector", lambda e, half=half: e.tensor_add(
                            macc[:, half * 24:half * 24 + 24],
                            macc[:, half * 24:half * 24 + 24], psum[6][:, 0:24]),
                         reads=[("pb", 6), mk], writes=[mk])

        def emit_mod_right_final():
            t_copy("vector", sb_cols[:, :], macc[:, 24:32], [("macc", 1)], ["sb_cols"])
            P.op("vector", lambda e: e.tensor_scalar(
                    cb1_cols[:, :], macc[:, 32:40], 1.0, None, ALU.add),
                 reads=[("macc", 1)], writes=["cb1_cols"])
            clamp_cols(gb_cols, "gb_cols", 40)
            cols_to_tile(gb_cols, "gb_cols", gb_tile, "gb_tile")
            P.op("vector", lambda e: e.tensor_mul(ctmp2[:, :], gb_cols[:, :], b2_cols[:, :]),
                 reads=["gb_cols", "b2_cols"], writes=["ctmp2"])
            cols_to_tile(ctmp2, "ctmp2", gbb2_tile, "gbb2_tile")

        with ExitStack() as pctx:
            cstage = alloc(pctx, "cstage", 8, 128)
            b2stage = alloc(pctx, "b2stage", 8, 128)
            vstage = alloc(pctx, "vstage", 32, 4 * 128)
            th_cols = alloc(pctx, "th_cols", 128, 32)
            wr_cols = alloc(pctx, "wr_cols", 128, 32)
            wi_cols = alloc(pctx, "wi_cols", 128, 32)
            beta_cols = alloc(pctx, "beta_cols", 128, 32)
            tmpc = alloc(pctx, "tmpc", 128, 32)
            tmpc2 = alloc(pctx, "tmpc2", 128, 32)
            sth_cols = alloc(pctx, "sth_cols", 128, 32)
            cth_cols = alloc(pctx, "cth_cols", 128, 32)
            a_cols = alloc(pctx, "a_cols", 128, 32)

            for name, dst in [("ones_row", ones_row), ("mask4_t", mask4_t),
                              ("id_bf", id_bf), ("id_f32", id_f32),
                              ("halfpi_col", halfpi_col), ("tenth_col", tenth_col),
                              ("qscale_t", qscale_t), ("kscale_t", kscale_t)]:
                P.dma_in(dst[:, :], cst_h[name], writes=[name])
            P.dma_in(cstage[:, :], c_h, writes=["cstage"])
            P.dma_in(b2stage[:, :], b2_h, writes=["b2stage"])
            P.dma_in(vstage[:, 0:128], b1_h, writes=["vs_b1"])
            P.dma_in(vstage[:, 128:256], th_h, writes=["vs_th"])
            P.dma_in(vstage[:, 256:384], wr_h, writes=["vs_wr"])
            P.dma_in(vstage[:, 384:512], wi_h, writes=["vs_wi"])

            P.op("tensor", lambda e: e.matmul(psum[7][:, 0:8], cstage[:, :],
                                              id_f32[0:8, 0:8], start=True, stop=True),
                 reads=["cstage", "id_f32"], writes=[("pb", 7)])
            P.op("scalar", lambda e: e.activation(silu_cols[:, :], psum[7][:, 0:8], AF.Silu),
                 reads=[("pb", 7)], writes=["silu_cols"])
            P.op("tensor", lambda e: e.matmul(psum[7][:, 8:16], b2stage[:, :],
                                              id_f32[0:8, 0:8], start=True, stop=True),
                 reads=["b2stage", "id_f32"], writes=[("pb", 7)])
            t_copy("vector", b2_cols[:, :], psum[7][:, 8:16], [("pb", 7)], ["b2_cols"])
            for j, key in enumerate(["vs_b1", "vs_th", "vs_wr", "vs_wi"]):
                P.op("tensor", lambda e, j=j: e.matmul(
                        psum[7][:, 32 + j * 32:32 + (j + 1) * 32],
                        vstage[:, j * 128:(j + 1) * 128], id_f32[0:32, 0:32],
                        start=True, stop=True),
                     reads=[key, "id_f32"], writes=[("pb", 7)])
            t_copy("vector", b1_cols[:, :], psum[7][:, 32:64], [("pb", 7)], ["b1_cols"])
            t_copy("vector", th_cols[:, :], psum[7][:, 64:96], [("pb", 7)], ["th_cols"])
            t_copy("vector", wr_cols[:, :], psum[7][:, 96:128], [("pb", 7)], ["wr_cols"])
            t_copy("vector", wi_cols[:, :], psum[7][:, 128:160], [("pb", 7)], ["wi_cols"])

            emit_matvec_half(0)
            for i in range(2):
                P.dma_cast(xbf[:, i * 1024:(i + 1) * 1024],
                           X_h[i * 128:(i + 1) * 128, :], writes=[("xbf", i)])
            wsrc = {"wq": Wq_h, "wk": Wk_h, "wv": Wv_h, "wg": Wg_h}
            for nm in ["wq", "wk", "wv", "wg"]:
                for dk in range(8):
                    P.dma_cast(wbf[nm][:, dk * 1024:(dk + 1) * 1024],
                               wsrc[nm][dk * 128:(dk + 1) * 128, :],
                               writes=[(nm, dk)])
            for i in range(2, 16):
                P.dma_cast(xbf[:, i * 1024:(i + 1) * 1024],
                           X_h[i * 128:(i + 1) * 128, :], writes=[("xbf", i)])

            if debug:
                P.dma_out(dbg_h["d_silu"], silu_cols[:, :], reads=["silu_cols"])
                P.dma_out(dbg_h["d_modc"][:, 0:24], macc[:, 0:24], reads=[("macc", 0)])
            t_copy("vector", sa_cols[:, :], macc[:, 0:8], [("macc", 0)], ["sa_cols"])
            P.op("vector", lambda e: e.tensor_scalar(
                    ca1_cols[:, :], macc[:, 8:16], 1.0, None, ALU.add),
                 reads=[("macc", 0)], writes=["ca1_cols"])
            clamp_cols(ga_cols, "ga_cols", 16)
            cols_to_tile(ga_cols, "ga_cols", ga_tile, "ga_tile")

            for dk in range(8):
                for hf in range(2):
                    stg = (dk * 2 + hf) % 2
                    P.dma_in(wstage[:, stg * 512:(stg + 1) * 512],
                             Wo_h[dk * 128:(dk + 1) * 128, hf * 512:(hf + 1) * 512],
                             writes=[("wstg", stg)])
                    P.op("vector", lambda e, dk=dk, stg=stg, hf=hf: e.tensor_mul(
                            wbf["wo"][:, dk * 1024 + hf * 512:dk * 1024 + (hf + 1) * 512],
                            wstage[:, stg * 512:(stg + 1) * 512],
                            ga_tile[:, hf * 512:(hf + 1) * 512]),
                         reads=[("wstg", stg), "ga_tile"], writes=[("wo", dk, hf)])

            def reduce_range(key, buf):
                for _ in range(8):
                    P.op("vector", lambda e: e.tensor_scalar(
                            tmpc[:, :], buf[:, :], PI, None, ALU.is_gt),
                         reads=[key], writes=["tmpc"])
                    P.op("vector", lambda e: e.scalar_tensor_tensor(
                            buf[:, :], tmpc[:, :], -2.0 * PI, buf[:, :], ALU.mult, ALU.add),
                         reads=["tmpc", key], writes=[key])
                    P.op("vector", lambda e: e.tensor_scalar(
                            tmpc[:, :], buf[:, :], -PI, None, ALU.is_lt),
                         reads=[key], writes=["tmpc"])
                    P.op("vector", lambda e: e.scalar_tensor_tensor(
                            buf[:, :], tmpc[:, :], 2.0 * PI, buf[:, :], ALU.mult, ALU.add),
                         reads=["tmpc", key], writes=[key])

            def cos_of(key, buf, outbuf, outkey):
                P.op("vector", lambda e: e.tensor_scalar(
                        tmpc2[:, :], buf[:, :], PI / 2.0, None, ALU.add),
                     reads=[key], writes=["tmpc2"])
                P.op("vector", lambda e: e.tensor_scalar(
                        tmpc[:, :], tmpc2[:, :], PI, None, ALU.is_gt),
                     reads=["tmpc2"], writes=["tmpc"])
                P.op("vector", lambda e: e.scalar_tensor_tensor(
                        tmpc2[:, :], tmpc[:, :], -2.0 * PI, tmpc2[:, :], ALU.mult, ALU.add),
                     reads=["tmpc", "tmpc2"], writes=["tmpc2"])
                P.op("scalar", lambda e: e.activation(outbuf[:, :], tmpc2[:, :], AF.Sin),
                     reads=["tmpc2"], writes=[outkey])

            P.op("vector", lambda e: e.scalar_tensor_tensor(
                    beta_cols[:, :], b1_cols[:, :], 0.1, th_cols[:, :], ALU.mult, ALU.add),
                 reads=["b1_cols", "th_cols"], writes=["beta_cols"])
            reduce_range("beta_cols", beta_cols)
            reduce_range("th_cols", th_cols)
            P.op("scalar", lambda e: e.activation(sth_cols[:, :], th_cols[:, :], AF.Sin),
                 reads=["th_cols"], writes=["sth_cols"])
            cos_of("th_cols", th_cols, cth_cols, "cth_cols")
            P.op("vector", lambda e: e.tensor_mul(cth_cols[:, :], cth_cols[:, :], wr_cols[:, :]),
                 reads=["cth_cols", "wr_cols"], writes=["cth_cols"])
            P.op("vector", lambda e: e.tensor_mul(sth_cols[:, :], sth_cols[:, :], wi_cols[:, :]),
                 reads=["sth_cols", "wi_cols"], writes=["sth_cols"])
            P.op("vector", lambda e: e.tensor_add(a_cols[:, :], cth_cols[:, :], sth_cols[:, :]),
                 reads=["cth_cols", "sth_cols"], writes=["a_cols"])
            P.op("vector", lambda e: e.scalar_tensor_tensor(
                    a_cols[:, :], a_cols[:, :], 0.1, wi_cols[:, :], ALU.mult, ALU.mult),
                 reads=["a_cols", "wi_cols"], writes=["a_cols"])
            P.op("scalar", lambda e: e.activation(sth_cols[:, :], beta_cols[:, :], AF.Sin),
                 reads=["beta_cols"], writes=["sth_cols"])
            cos_of("beta_cols", beta_cols, cth_cols, "cth_cols")
            P.op("vector", lambda e: e.tensor_mul(p_cols[:, :], a_cols[:, :], sth_cols[:, :]),
                 reads=["a_cols", "sth_cols"], writes=["p_cols"])
            P.op("vector", lambda e: e.tensor_mul(r_cols[:, :], a_cols[:, :], cth_cols[:, :]),
                 reads=["a_cols", "cth_cols"], writes=["r_cols"])

            if debug:
                P.dma_out(dbg_h["d_pcols"], p_cols[:, :], reads=["p_cols"])
                P.dma_out(dbg_h["d_rcols"], r_cols[:, :], reads=["r_cols"])
                P.dma_out(dbg_h["d_ga"], ga_tile[:, :], reads=["ga_tile"])

            if upto == "P":
                P.final_join()
            with nc.Block("blk_p") as blk:
                P.flush(blk)
            if upto == "P":
                return P.usage

        with ExitStack() as actx:

            def stage_Pstats(g):
                for tsub in range(2):
                    i = 2 * g + tsub
                    sk = ("ln1", g)
                    P.op("vector", lambda e, i=i, tsub=tsub: e.tensor_reduce(
                            stats[:, tsub:tsub + 1], xbf[:, i * 1024:(i + 1) * 1024],
                            AX.X, ALU.add),
                         reads=[("xbf", i), "stats_a"],
                         writes=[(sk, "s%d" % tsub), "stats_a"])
                    P.op("scalar", lambda e, i=i, tsub=tsub: e.activation(
                            sqscr[:, :], xbf[:, i * 1024:(i + 1) * 1024], AF.Square,
                            accum_out=stats[:, 2 + tsub:3 + tsub]),
                         reads=[("xbf", i), "stats_a", "sqscr"],
                         writes=[(sk, "q%d" % tsub), "sqscr", "stats_a"])
                    if tsub == 1:
                        ln_chain(sk, 1e-5)
                for tsub in range(2):
                    i = 2 * g + tsub
                    P.op("vector", lambda e, i=i, tsub=tsub: e.tensor_scalar(
                            hpre[:, tsub * 1024:(tsub + 1) * 1024],
                            xbf[:, i * 1024:(i + 1) * 1024],
                            stats[:, 4 + tsub:5 + tsub], stats[:, 6 + tsub:7 + tsub],
                            ALU.subtract, ALU.mult),
                         reads=[("xbf", i), (sk, "y"), "stats_a", ("hpre", tsub)],
                         writes=[("hpre", tsub)])

            def stage_Pmm(g):
                sl = g % 2
                h1keys = [("h1", dk) for dk in range(8)]
                sk = ("ln1", g)
                for tsub in range(2):
                    i = 2 * g + tsub
                    hs = tsub * 1024
                    hk = ("hpre", tsub)
                    for dk in range(8):
                        pb = dk % 2
                        P.op("tensor", lambda e, dk=dk, pb=pb, hs=hs: e.matmul(
                                psum[pb][:, 0:128],
                                hpre[:, hs + dk * 128:hs + (dk + 1) * 128],
                                id_bf[:, :], start=True, stop=True),
                             reads=[hk, "id_bf"], writes=[("pb", pb)])
                        P.op("vector", lambda e, tsub=tsub, dk=dk, pb=pb: e.tensor_scalar(
                                h1r[:, dk * 256 + tsub * 128:
                                     dk * 256 + (tsub + 1) * 128],
                                psum[pb][:, 0:128],
                                ca1_cols[:, dk:dk + 1], sa_cols[:, dk:dk + 1],
                                ALU.mult, ALU.add),
                             reads=[("pb", pb), "ca1_cols", "sa_cols"],
                             writes=[("h1", dk)])

                for wnm, dstbuf, dkey, scaled in [("wq", qtr, "qt", True),
                                                  ("wk", ktr, "kt", False)]:
                    for quad in range(2):
                        for hh in range(4):
                            h = quad * 4 + hh
                            pb = (0 if quad == 0 else 2) + hh // 2
                            r0 = (hh % 2) * 256
                            for dk in range(8):
                                P.op("tensor", lambda e, wnm=wnm, h=h, dk=dk, pb=pb,
                                        r0=r0: e.matmul(
                                        psum[pb][:, r0:r0 + 256],
                                        wbf[wnm][:, dk * 1024 + h * 128:
                                                 dk * 1024 + (h + 1) * 128],
                                        h1r[:, dk * 256:(dk + 1) * 256],
                                        start=(dk == 0), stop=(dk == 7)),
                                     reads=[(wnm, dk)] + h1keys, writes=[("pb", pb)])
                        for bi in range(2):
                            pb = (0 if quad == 0 else 2) + bi
                            col = quad * 1024 + bi * 512
                            sct = qscale_t if scaled else kscale_t
                            sck = "qscale_t" if scaled else "kscale_t"
                            P.op("vector", lambda e, pb=pb, col=col, sl=sl, dstbuf=dstbuf,
                                    sct=sct: e.tensor_mul(
                                    dstbuf[:, sl * 2048 + col:sl * 2048 + col + 512],
                                    psum[pb][:, 0:512], sct[:, col:col + 512]),
                                 reads=[("pb", pb), sck],
                                 writes=[(dkey, sl, quad * 2 + bi)])

                for wnm, dstbuf, dkey, actf in [("wv", vtr, "vt", AF.Copy),
                                                ("wg", gtr, "gt", AF.Silu)]:
                    for tsub in range(2):
                        pbs = (0, 1) if tsub == 0 else (2, 3)
                        for half in range(2):
                            pb = pbs[half]
                            for dk in range(8):
                                P.op("tensor", lambda e, wnm=wnm, dk=dk, pb=pb, half=half,
                                        tsub=tsub: e.matmul(
                                        psum[pb][:, 0:512],
                                        h1r[:, dk * 256 + tsub * 128:
                                             dk * 256 + (tsub + 1) * 128],
                                        wbf[wnm][:, dk * 1024 + half * 512:
                                                dk * 1024 + (half + 1) * 512],
                                        start=(dk == 0), stop=(dk == 7)),
                                     reads=[(wnm, dk)] + h1keys, writes=[("pb", pb)])
                            P.op("scalar", lambda e, pb=pb, half=half, tsub=tsub,
                                    dstbuf=dstbuf, actf=actf: e.activation(
                                    dstbuf[:, tsub * 1024 + half * 512:
                                           tsub * 1024 + (half + 1) * 512],
                                    psum[pb][:, 0:512], actf),
                                 reads=[("pb", pb)],
                                 writes=[(dkey, 0, tsub * 2 + half)])

                if debug:
                    for dk in range(8):
                        P.dma_out(dbg_h["d_h1"][dk * 128:(dk + 1) * 128,
                                                g * 256:(g + 1) * 256],
                                  h1r[:, dk * 256:(dk + 1) * 256],
                                  reads=[("h1", dk)])
                    for h in range(H):
                        P.dma_out(dbg_h["d_qt"][h * 128:(h + 1) * 128, g * 256:(g + 1) * 256],
                                  qtr[:, sl * 2048 + h * 256:sl * 2048 + (h + 1) * 256],
                                  reads=[("qt", sl, j) for j in range(4)])
                        P.dma_out(dbg_h["d_kt"][h * 128:(h + 1) * 128, g * 256:(g + 1) * 256],
                                  ktr[:, sl * 2048 + h * 256:sl * 2048 + (h + 1) * 256],
                                  reads=[("kt", sl, j) for j in range(4)])
                    for tsub in range(2):
                        i = 2 * g + tsub
                        P.dma_out(dbg_h["d_v"][i * 128:(i + 1) * 128, :],
                                  vtr[:, tsub * 1024:(tsub + 1) * 1024],
                                  reads=[("vt", 0, tsub * 2), ("vt", 0, tsub * 2 + 1)])
                        P.dma_out(dbg_h["d_g"][i * 128:(i + 1) * 128, :],
                                  gtr[:, tsub * 1024:(tsub + 1) * 1024],
                                  reads=[("gt", 0, tsub * 2), ("gt", 0, tsub * 2 + 1)])

            def stage_R(g):
                sl = g % 2
                qkeys = [("qt", sl, j) for j in range(4)]
                kkeys = [("kt", sl, j) for j in range(4)]
                rbase = {0: 4, 1: 2}

                def ktm_scores(tsub):
                    for h in range(H):
                        pb = 6 + h // 4
                        P.op("tensor", lambda e, h=h, pb=pb, sl=sl, tsub=tsub: e.matmul(
                                psum[pb][:, (h % 4) * 128:(h % 4 + 1) * 128],
                                ktr[:, sl * 2048 + h * 256 + tsub * 128:
                                     sl * 2048 + h * 256 + (tsub + 1) * 128],
                                id_bf[:, :], start=True, stop=True),
                             reads=kkeys + ["id_bf"], writes=[("pb", pb)])
                    for bi in range(2):
                        pb = 6 + bi
                        P.op("vector", lambda e, pb=pb, bi=bi: e.tensor_copy(
                                ktm[:, bi * 512:(bi + 1) * 512],
                                psum[pb][:, 0:512]),
                             reads=[("pb", pb)], writes=[("ktm", bi)])
                    for h in range(H):
                        pb = 6 + h // 4
                        P.op("tensor", lambda e, h=h, pb=pb, sl=sl, tsub=tsub: e.matmul(
                                psum[pb][:, (h % 4) * 128:(h % 4 + 1) * 128],
                                ktr[:, sl * 2048 + h * 256 + tsub * 128:
                                     sl * 2048 + h * 256 + (tsub + 1) * 128],
                                qtr[:, sl * 2048 + h * 256 + tsub * 128:
                                     sl * 2048 + h * 256 + (tsub + 1) * 128],
                                start=True, stop=True),
                             reads=kkeys + qkeys, writes=[("pb", pb)])
                    for bi in range(2):
                        pb = 6 + bi
                        P.op("vector", lambda e, pb=pb, bi=bi: e.tensor_mul(
                                abuf[:, bi * 512:(bi + 1) * 512],
                                psum[pb][:, 0:512], mask4_t[:, :]),
                             reads=[("pb", pb), "mask4_t"], writes=[("ab", bi)])

                def retv_ktv(tsub):
                    i = 2 * g + tsub
                    mv = (i > 0)
                    rb = rbase[tsub]
                    vkeys = [("vt", 0, tsub * 2), ("vt", 0, tsub * 2 + 1)]
                    for h in range(H):
                        pb = rb + h // 4
                        P.op("tensor", lambda e, h=h, pb=pb, tsub=tsub,
                                mv=mv: e.matmul(
                                psum[pb][:, (h % 4) * 128:(h % 4 + 1) * 128],
                                abuf[:, h * 128:(h + 1) * 128],
                                vtr[:, tsub * 1024 + h * 128:
                                     tsub * 1024 + (h + 1) * 128],
                                start=True, stop=not mv),
                             reads=[("ab", h // 4)] + vkeys, writes=[("pb", pb)])
                        if mv:
                            P.op("tensor", lambda e, h=h, pb=pb, sl=sl, tsub=tsub: e.matmul(
                                    psum[pb][:, (h % 4) * 128:(h % 4 + 1) * 128],
                                    qtr[:, sl * 2048 + h * 256 + tsub * 128:
                                         sl * 2048 + h * 256 + (tsub + 1) * 128],
                                    Mbf[:, h * 128:(h + 1) * 128],
                                    start=False, stop=True),
                                 reads=qkeys + [("Mbf", h)], writes=[("pb", pb)])
                    for h in range(H):
                        pb = 6 + h // 4
                        P.op("tensor", lambda e, h=h, pb=pb, tsub=tsub: e.matmul(
                                psum[pb][:, (h % 4) * 128:(h % 4 + 1) * 128],
                                ktm[:, h * 128:(h + 1) * 128],
                                vtr[:, tsub * 1024 + h * 128:
                                     tsub * 1024 + (h + 1) * 128],
                                start=True, stop=True),
                             reads=[("ktm", h // 4)] + vkeys, writes=[("pb", pb)])

                    for h in range(H):
                        pb = 6 + h // 4
                        if mv:
                            P.op("vector", lambda e, h=h, pb=pb: e.scalar_tensor_tensor(
                                    Mst[:, h * 128:(h + 1) * 128],
                                    Mst[:, h * 128:(h + 1) * 128], GAMMA_C[h],
                                    psum[pb][:, (h % 4) * 128:(h % 4 + 1) * 128],
                                    ALU.mult, ALU.add),
                                 reads=[("pb", pb), ("Mst", h)], writes=[("Mst", h)])
                        else:
                            P.op("vector", lambda e, h=h, pb=pb: e.tensor_copy(
                                    Mst[:, h * 128:(h + 1) * 128],
                                    psum[pb][:, (h % 4) * 128:(h % 4 + 1) * 128]),
                                 reads=[("pb", pb)], writes=[("Mst", h)])
                        P.op("scalar", lambda e, h=h: e.activation(
                                Mbf[:, h * 128:(h + 1) * 128],
                                Mst[:, h * 128:(h + 1) * 128],
                                AF.Identity, scale=GAMMA_C[h]),
                             reads=[("Mst", h)], writes=[("Mbf", h)])

                def gn_gate(tsub):
                    i = 2 * g + tsub
                    rb = rbase[tsub]
                    gk = ("gn", i)
                    for bi in range(2):
                        pb = rb + bi
                        P.op("vector", lambda e, pb=pb, bi=bi: e.tensor_reduce(
                                stats[:, 8 + bi * 4:8 + (bi + 1) * 4],
                                psum[pb][:, 0:512].rearrange("p (h x) -> p h x", x=128),
                                AX.X, ALU.add),
                             reads=[("pb", pb), "stats_b"],
                             writes=[(gk, "s"), "stats_b"])
                        P.op("scalar", lambda e, pb=pb, bi=bi: e.activation(
                                sqscr[:, bi * 512:(bi + 1) * 512], psum[pb][:, 0:512],
                                AF.Square),
                             reads=[("pb", pb), "sqscr"], writes=["sqscr"])
                        P.op("vector", lambda e, bi=bi: e.tensor_reduce(
                                stats[:, 16 + bi * 4:16 + (bi + 1) * 4],
                                sqscr[:, bi * 512:(bi + 1) * 512]
                                .rearrange("p (h x) -> p h x", x=128),
                                AX.X, ALU.add),
                             reads=["sqscr", "stats_b"], writes=[(gk, "q"), "stats_b"])
                    P.op("vector", lambda e: e.tensor_scalar(
                            stats[:, 8:16], stats[:, 8:16], 1.0 / DH, None, ALU.mult),
                         reads=[(gk, "s"), "stats_b"], writes=[(gk, "mu"), "stats_b"])
                    P.op("vector", lambda e: e.tensor_scalar(
                            stats[:, 16:24], stats[:, 16:24], 1.0 / DH, None, ALU.mult),
                         reads=[(gk, "q"), "stats_b"], writes=[(gk, "msq"), "stats_b"])
                    P.op("vector", lambda e: e.scalar_tensor_tensor(
                            stats[:, 40:48], stats[:, 8:16], -1.0, stats[:, 8:16],
                            ALU.mult, ALU.mult),
                         reads=[(gk, "mu"), "stats_b"], writes=[(gk, "m2"), "stats_b"])
                    P.op("vector", lambda e: e.tensor_add(
                            stats[:, 40:48], stats[:, 40:48], stats[:, 16:24]),
                         reads=[(gk, "m2"), (gk, "msq"), "stats_b"],
                         writes=[(gk, "var"), "stats_b"])
                    P.op("vector", lambda e: e.tensor_scalar(
                            stats[:, 40:48], stats[:, 40:48], 1e-6, None, ALU.add),
                         reads=[(gk, "var"), "stats_b"], writes=[(gk, "veps"), "stats_b"])
                    rsqrt_cols(gk, "stats_b", (40, 48), (16, 24), (24, 32))
                    P.op("vector", lambda e: e.scalar_tensor_tensor(
                            stats[:, 8:16], stats[:, 8:16], -1.0, stats[:, 16:24],
                            ALU.mult, ALU.mult),
                         reads=[(gk, "mu"), (gk, "y"), "stats_b"],
                         writes=[(gk, "mrs"), "stats_b"])
                    for h in range(H):
                        pb = rb + h // 4
                        P.op("scalar", lambda e, h=h, pb=pb: e.activation(
                                retn[:, h * 128:(h + 1) * 128],
                                psum[pb][:, (h % 4) * 128:(h % 4 + 1) * 128], AF.Identity,
                                bias=stats[:, 8 + h:9 + h], scale=stats[:, 16 + h:17 + h]),
                             reads=[("pb", pb), (gk, "mrs"), (gk, "y"), "stats_b"],
                             writes=["retn"])
                    if debug:
                        P.dma_out(dbg_h["d_ret"][i * 128:(i + 1) * 128, :],
                                  retn[:, :], reads=["retn"])
                        P.dma_out(dbg_h["d_mbf"][i * 128:(i + 1) * 128, :],
                                  Mbf[:, :], reads=[("Mbf", h) for h in range(H)])
                    P.op("gpsimd", lambda e, tsub=tsub: e.tensor_mul(
                            gated[:, tsub * 1024:(tsub + 1) * 1024],
                            retn[:, 0:1024],
                            gtr[:, tsub * 1024:(tsub + 1) * 1024]),
                         reads=["retn", ("gt", 0, tsub * 2), ("gt", 0, tsub * 2 + 1)],
                         writes=[("gated", tsub)])

                ktm_scores(0)
                retv_ktv(0)
                ktm_scores(1)
                retv_ktv(1)
                gn_gate(0)
                gn_gate(1)
            def stage_W(g):
                for tsub in range(2):
                    i = 2 * g + tsub
                    for dk in range(8):
                        pb = 2 + (dk % 2)
                        rg = ((dk // 2) % 4)
                        P.op("tensor", lambda e, dk=dk, pb=pb, rg=rg, tsub=tsub: e.matmul(
                                psum[pb][:, rg * 128:(rg + 1) * 128],
                                gated[:, tsub * 1024 + dk * 128:
                                      tsub * 1024 + (dk + 1) * 128],
                                id_bf[:, :], start=True, stop=True),
                             reads=[("gated", tsub), "id_bf"], writes=[("pb", pb)])
                        P.op("vector", lambda e, dk=dk, pb=pb, rg=rg: e.tensor_copy(
                                gfm[:, dk * 128:(dk + 1) * 128],
                                psum[pb][:, rg * 128:(rg + 1) * 128]),
                             reads=[("pb", pb)], writes=["gfm"])
                    for half in range(2):
                        pb = half
                        for dk in range(8):
                            P.op("tensor", lambda e, dk=dk, pb=pb, half=half: e.matmul(
                                    psum[pb][:, 0:512],
                                    gfm[:, dk * 128:(dk + 1) * 128],
                                    wbf["wo"][:, dk * 1024 + half * 512:
                                              dk * 1024 + (half + 1) * 512],
                                    start=(dk == 0), stop=(dk == 7)),
                                 reads=["gfm", ("wo", dk, half)], writes=[("pb", pb)])
                        P.op("vector", lambda e, pb=pb, half=half, i=i: e.tensor_add(
                                xbf[:, i * 1024 + half * 512:i * 1024 + (half + 1) * 512],
                                psum[pb][:, 0:512],
                                xbf[:, i * 1024 + half * 512:i * 1024 + (half + 1) * 512]),
                             reads=[("pb", pb), ("xbf", i)], writes=[("xbf", i)])
                    if debug:
                        P.dma_out(dbg_h["d_x2"][i * 128:(i + 1) * 128, :],
                                  xbf[:, i * 1024:(i + 1) * 1024],
                                  reads=[("xbf", i)])

            if upto.startswith(("A", "R")):
                stage_Pstats(0)
                stage_Pmm(0)
                P.final_join()
                with nc.Block("blk_a") as blk:
                    P.flush(blk)
                return P.usage

            mod_pieces = {0: [0], 1: [1, 2], 2: [3], 3: [4], 4: [5], 5: [6, 7]}
            stage_Pstats(0)
            for g in range(NG):
                if g >= 2:
                    stage_W(g - 2)
                if g >= 1:
                    stage_R(g - 1)
                stage_Pmm(g)
                if g + 1 < NG:
                    stage_Pstats(g + 1)
                if g in mod_pieces:
                    emit_matvec_half(1, dks=mod_pieces[g])
            stage_W(NG - 2)
            stage_R(NG - 1)
            stage_W(NG - 1)
            emit_mod_right_final()

            with nc.Block("blk_a") as blk:
                P.flush(blk)
        ab_ctx.close()
        P.fence()

        with ExitStack() as bctx:
            w1bf = alloc(bctx, "w1bf", 128, 8 * 4096, BF16)
            w2bf = alloc(bctx, "w2bf", 128, 32 * 1024, BF16)
            h2r = alloc(bctx, "h2r", 128, 2 * 2048, BF16)
            h2pre = alloc(bctx, "h2pre", 128, 1024, BF16)
            actb = alloc(bctx, "actb", 128, 4 * 256, BF16)
            csin = alloc(bctx, "csin", 128, 4 * 512, BF16)
            t1b = alloc(bctx, "t1b", 128, 4 * 256)
            outr = alloc(bctx, "outr", 128, 1024)

            for qt in range(4):
                for dk in range(8):
                    P.dma_cast(w1bf[:, dk * 4096 + qt * 1024:dk * 4096 + (qt + 1) * 1024],
                               W1_h[dk * 128:(dk + 1) * 128, qt * 1024:(qt + 1) * 1024],
                               writes=[("w1g", qt)], chain=(dk == 0), sem_idx=0,
                               force_inc=True)
                for fk8 in range(8):
                    fk = qt * 8 + fk8
                    P.dma_cast(w2bf[:, fk * 1024:(fk + 1) * 1024],
                               W2_h[fk * 128:(fk + 1) * 128, :],
                               writes=[("w2g", qt)], chain=(fk8 == 0), sem_idx=1,
                               force_inc=True)

            def ln2_stats(tau):
                for tsub in range(2):
                    i = 2 * tau + tsub
                    sk = ("ln2", tau)
                    P.op("vector", lambda e, i=i, tsub=tsub: e.tensor_reduce(
                            stats[:, tsub:tsub + 1], xbf[:, i * 1024:(i + 1) * 1024],
                            AX.X, ALU.add),
                         reads=[("xbf", i), "stats_a"],
                         writes=[(sk, "s%d" % tsub), "stats_a"])
                    P.op("scalar", lambda e, i=i, tsub=tsub: e.activation(
                            sqscr[:, :], xbf[:, i * 1024:(i + 1) * 1024], AF.Square,
                            accum_out=stats[:, 2 + tsub:3 + tsub]),
                         reads=[("xbf", i), "stats_a", "sqscr"],
                         writes=[(sk, "q%d" % tsub), "sqscr", "stats_a"])
                    if tsub == 1:
                        ln_chain(sk, 1e-5)

            def ln2_tr(tau):
                sl = tau % 2
                for tsub in range(2):
                    i = 2 * tau + tsub
                    sk = ("ln2", tau)
                    P.op("vector", lambda e, i=i, tsub=tsub: e.tensor_scalar(
                            h2pre[:, :], xbf[:, i * 1024:(i + 1) * 1024],
                            stats[:, 4 + tsub:5 + tsub], stats[:, 6 + tsub:7 + tsub],
                            ALU.subtract, ALU.mult),
                         reads=[("xbf", i), (sk, "y"), "stats_a", "h2pre"],
                         writes=["h2pre"])
                    for dk in range(8):
                        pb = 6 + (dk % 2)
                        P.op("tensor", lambda e, dk=dk, pb=pb: e.matmul(
                                psum[pb][:, 0:128],
                                h2pre[:, dk * 128:(dk + 1) * 128],
                                id_bf[:, :], start=True, stop=True),
                             reads=["h2pre", "id_bf"], writes=[("pb", pb)])
                        P.op("vector", lambda e, sl=sl, tsub=tsub, dk=dk, pb=pb:
                                e.tensor_scalar(
                                h2r[:, sl * 2048 + dk * 256 + tsub * 128:
                                     sl * 2048 + dk * 256 + (tsub + 1) * 128],
                                psum[pb][:, 0:128],
                                cb1_cols[:, dk:dk + 1], sb_cols[:, dk:dk + 1],
                                ALU.mult, ALU.add),
                             reads=[("pb", pb), "cb1_cols", "sb_cols"],
                             writes=[("h2", sl, dk)])
                if debug:
                    for dk in range(8):
                        P.dma_out(dbg_h["d_h2"][dk * 128:(dk + 1) * 128,
                                                tau * 256:(tau + 1) * 256],
                                  h2r[:, sl * 2048 + dk * 256:sl * 2048 + (dk + 1) * 256],
                                  reads=[("h2", sl, dk)])

            for tau in range(NTT):
                sl = tau % 2
                h2keys = [("h2", sl, dk) for dk in range(8)]
                if tau == 0:
                    ln2_stats(0)
                    ln2_tr(0)

                def w1_mm(fk):
                    pbh = fk % 2
                    for dk in range(8):
                        P.op("tensor", lambda e, fk=fk, dk=dk, pbh=pbh, sl=sl: e.matmul(
                                psum[pbh][:, 0:256],
                                w1bf[:, dk * 4096 + fk * 128:dk * 4096 + (fk + 1) * 128],
                                h2r[:, sl * 2048 + dk * 256:sl * 2048 + (dk + 1) * 256],
                                start=(dk == 0), stop=(dk == 7)),
                             reads=[("w1g", fk // 8)] + h2keys, writes=[("pb", pbh)])

                def mid(fk):
                    pbh = fk % 2
                    cs = fk % 4
                    P.op("scalar", lambda e, pbh=pbh, cs=cs: e.activation(
                            csin[:, cs * 512:cs * 512 + 256], psum[pbh][:, 0:256], AF.Sin,
                            bias=halfpi_col[:, 0:1], scale=tenth_col[:, 0:1]),
                         reads=[("pb", pbh), "halfpi_col", "tenth_col"],
                         writes=[("csin", cs)])
                    P.op("scalar", lambda e, pbh=pbh, cs=cs: e.activation(
                            csin[:, cs * 512 + 256:cs * 512 + 512], psum[pbh][:, 0:256],
                            AF.Sin, scale=tenth_col[:, 0:1]),
                         reads=[("pb", pbh), "tenth_col"], writes=[("ssin", cs)])
                    P.op("vector", lambda e, pbh=pbh, cs=cs, fk=fk: e.scalar_tensor_tensor(
                            t1b[:, cs * 256:(cs + 1) * 256],
                            csin[:, cs * 512:cs * 512 + 256], p_cols[:, fk:fk + 1],
                            psum[pbh][:, 0:256], ALU.mult, ALU.add),
                         reads=[("csin", cs), ("pb", pbh), "p_cols"], writes=[("t1", cs)])
                    P.op("vector", lambda e, cs=cs, fk=fk: e.scalar_tensor_tensor(
                            t1b[:, cs * 256:(cs + 1) * 256],
                            csin[:, cs * 512 + 256:cs * 512 + 512], r_cols[:, fk:fk + 1],
                            t1b[:, cs * 256:(cs + 1) * 256], ALU.mult, ALU.add),
                         reads=[("ssin", cs), ("t1", cs), "r_cols"], writes=[("t1", cs)])
                    P.op("scalar", lambda e, cs=cs, fk=fk: e.activation(
                            actb[:, (fk % 4) * 256:(fk % 4 + 1) * 256],
                            t1b[:, cs * 256:(cs + 1) * 256], AF.Relu,
                            bias=b1_cols[:, fk:fk + 1]),
                         reads=[("t1", cs), "b1_cols"], writes=[("act", fk % 4)])

                def w2_mm(fk):
                    for tsub in range(2):
                        for half in range(2):
                            pb = 2 + tsub * 2 + half
                            P.op("tensor", lambda e, fk=fk, tsub=tsub, half=half, pb=pb:
                                    e.matmul(
                                    psum[pb][:, 0:512],
                                    actb[:, (fk % 4) * 256 + tsub * 128:
                                         (fk % 4) * 256 + (tsub + 1) * 128],
                                    w2bf[:, fk * 1024 + half * 512:
                                         fk * 1024 + (half + 1) * 512],
                                    start=(fk == 0), stop=(fk == 31)),
                                 reads=[("act", fk % 4), ("w2g", fk // 8)],
                                 writes=[("pb", pb)])

                for step in range(34):
                    if step >= 2:
                        w2_mm(step - 2)
                    if step < 32:
                        w1_mm(step)
                        mid(step)
                if tau + 1 < NTT:
                    ln2_stats(tau + 1)
                    ln2_tr(tau + 1)
                for tsub in range(2):
                    i = 2 * tau + tsub
                    for half in range(2):
                        pb = 2 + tsub * 2 + half
                        ok = (i * 2 + half) % 2
                        P.op("vector", lambda e, pb=pb, ok=ok, half=half: e.tensor_mul(
                                outr[:, ok * 512:(ok + 1) * 512],
                                psum[pb][:, 0:512],
                                gb_tile[:, half * 512:(half + 1) * 512]),
                             reads=[("pb", pb), "gb_tile"], writes=[("outr", ok)])
                        P.op("vector", lambda e, ok=ok, half=half: e.tensor_add(
                                outr[:, ok * 512:(ok + 1) * 512],
                                outr[:, ok * 512:(ok + 1) * 512],
                                gbb2_tile[:, half * 512:(half + 1) * 512]),
                             reads=["gbb2_tile", ("outr", ok)], writes=[("outr", ok)])
                        P.op("vector", lambda e, ok=ok, half=half, i=i: e.tensor_add(
                                outr[:, ok * 512:(ok + 1) * 512],
                                outr[:, ok * 512:(ok + 1) * 512],
                                xbf[:, i * 1024 + half * 512:i * 1024 + (half + 1) * 512]),
                             reads=[("xbf", i), ("outr", ok)],
                             writes=[("outr", ok)])
                        P.dma_out(out_h[i * 128:(i + 1) * 128, half * 512:(half + 1) * 512],
                                  outr[:, ok * 512:(ok + 1) * 512],
                                  reads=[("outr", ok)])

            P.final_join()
            with nc.Block("blk_b") as blk:
                P.flush(blk)

    return P.usage


_CACHED = {}


def kernel(**inputs):
    nc = _CACHED.get("nc")
    if nc is None:
        nc = build_program(debug=False)
        _CACHED["nc"] = nc
    cst = host_constants()
    X = np.ascontiguousarray(np.asarray(inputs["L0_attn_X"]))
    Cc = np.ascontiguousarray(np.asarray(inputs["L0_c"]))
    base = {}
    for k in ["L0_attn_Wq", "L0_attn_Wk", "L0_attn_Wv", "L0_attn_W_G", "L0_attn_W_O",
              "L0_ffn_W1", "L0_ffn_W2", "L0_wmod"]:
        base[k] = np.ascontiguousarray(np.asarray(inputs[k]))
    for k in ["L0_ffn_b1", "L0_ffn_quantum_phases", "L0_ffn_quantum_weights_real",
              "L0_ffn_quantum_weights_imag"]:
        base[k] = np.ascontiguousarray(np.asarray(inputs[k]).reshape(32, 128))
    base["L0_ffn_b2"] = np.ascontiguousarray(np.asarray(inputs["L0_ffn_b2"]).reshape(8, 128))
    for k, v in cst.items():
        base[k] = np.ascontiguousarray(v)
    in_maps = []
    for b in range(8):
        m = dict(base)
        m["L0_attn_X"] = np.ascontiguousarray(X[b])
        m["L0_c"] = np.ascontiguousarray(Cc[b].reshape(8, 128))
        in_maps.append(m)
    res = run_bass_kernel_spmd(nc, in_maps, core_ids=list(range(8)))
    return np.stack([np.asarray(r["OUT"]) for r in res.results], axis=0)
